# version 08
# baseline (speedup 1.0000x reference)
import numpy as np
import ml_dtypes
from contextlib import ExitStack

import concourse.bass as bass
import concourse.mybir as mybir
from concourse.bass_utils import run_bass_kernel_spmd
from concourse.aot_env import allow_ici_all_to_all

F32 = mybir.dt.float32
BF16 = mybir.dt.bfloat16
I32 = mybir.dt.int32
AF = mybir.ActivationFunctionType
ALU = mybir.AluOpType
AX = mybir.AxisListType
NPBF16 = ml_dtypes.bfloat16

C = 4096
T = 8192
HD = 128
NCORES = 8
HLOC = 4
TOWN = 1024
NCC = 32
SCALE = 1.0 / float(np.sqrt(HD))
ASCALE = 1.0 / 64.0
EPS = 1e-5
DBG = False

ENGS = ("sync", "tensor", "vector", "scalar", "gpsimd")

SEM_NAMES = ("pe", "act", "dve", "pool", "cc",
             "dC", "dx0", "dx1", "dx2", "dx3", "dsp", "da0", "da1", "dw0", "dw1", "do0", "do1", "do2", "do3",
             "dWq", "dt0", "dt1", "dr0", "dr1", "dr2", "dr3", "dv0", "dv1",
             "dn0", "dn1", "dh0", "dh1", "dy0", "dy1", "dxn", "dyt",
             "dp0", "dp1", "df0", "df1", "dz0", "dz1", "dD")

SCHED_TRACE = []


class Sched:

    def __init__(self, cnt):
        self.ops = {e: [] for e in ENGS}
        self.cnt = cnt

    def v(self, sem):
        return self.cnt.get(sem, 0)

    def add(self, eng, fn, waits=(), inc=None, n=1):
        self.ops[eng].append((tuple(waits), fn, inc, n))
        if inc is not None:
            self.cnt[inc] = self.v(inc) + n
            return self.cnt[inc]
        return None


def check_deadlock():
    sem = {}
    for name, ops in SCHED_TRACE:
        heads = {e: 0 for e in ops}
        total = sum(len(v) for v in ops.values())
        done = 0
        while done < total:
            progressed = False
            for e, lst in ops.items():
                while heads[e] < len(lst):
                    waits, inc, n = lst[heads[e]]
                    if all(sem.get(s, 0) >= v for (s, v) in waits if v > 0):
                        if inc is not None:
                            sem[inc] = sem.get(inc, 0) + n
                        heads[e] += 1
                        done += 1
                        progressed = True
                    else:
                        break
            if not progressed:
                stuck = {e: (heads[e], lst[heads[e]][0]) for e, lst in ops.items()
                         if heads[e] < len(lst)}
                detail = {e: [(s, v, sem.get(s, 0)) for (s, v) in w if v > 0]
                          for e, (i, w) in stuck.items()}
                raise RuntimeError(f"DEADLOCK in phase {name}: {detail}")
    return True


def emit_block(nc, sems, S, name):
    SCHED_TRACE.append((name, {e: [(w, inc, n) for (w, fn, inc, n) in ops]
                               for e, ops in S.ops.items() if ops}))
    with nc.Block(name, no_gpsimd_drain=True) as blk:
        for eng_name in ENGS:
            ops = S.ops[eng_name]
            if not ops:
                continue

            def make_body(_ops):
                def body(e):
                    for waits, fn, inc, n in _ops:
                        for (s, v) in waits:
                            if v > 0:
                                e.wait_ge(sems[s], v)
                        if fn is not None:
                            instr = fn(e)
                            if inc is not None:
                                instr.then_inc(sems[inc], n)
                return body

            getattr(blk, eng_name)(make_body(ops))


def build(dbg=False):
    nc = bass.Bass("TRN2", target_bir_lowering=False)

    xT_perm = nc.dram_tensor("xT_perm", [C, TOWN], F32, kind="ExternalInput")
    xT_nat = nc.dram_tensor("xT_nat", [C, TOWN], F32, kind="ExternalInput")
    watt_qkT = nc.dram_tensor("watt_qkT", [8 * C, 128], F32, kind="ExternalInput")
    wvT = nc.dram_tensor("wvT", [C, 512], F32, kind="ExternalInput")
    wakT = nc.dram_tensor("wakT", [C, 512], F32, kind="ExternalInput")
    wprojT = nc.dram_tensor("wprojT", [32 * C, 128], F32, kind="ExternalInput")
    wffn_pk = nc.dram_tensor("wffn_pk", [32 * 2048, 128], I32, kind="ExternalInput")
    zp_row = nc.dram_tensor("zp_row", [1, C], F32, kind="ExternalInput")
    scale_pc = nc.dram_tensor("scale_pc", [128, 32], F32, kind="ExternalInput")
    bias_pc = nc.dram_tensor("bias_pc", [128, 32], F32, kind="ExternalInput")
    adT = nc.dram_tensor("adT", [128, NCC * 10], F32, kind="ExternalInput")
    gating_in = nc.dram_tensor("gating_in", [1, 1], F32, kind="ExternalInput")
    cosT_in = nc.dram_tensor("cosT_in", [128, T], BF16, kind="ExternalInput")
    sinT_in = nc.dram_tensor("sinT_in", [128, T], BF16, kind="ExternalInput")
    trimask_in = nc.dram_tensor("trimask_in", [128, 128], BF16, kind="ExternalInput")
    ident_in = nc.dram_tensor("ident_in", [128, 128], BF16, kind="ExternalInput")

    outT = nc.dram_tensor("outT", [C, TOWN], F32, kind="ExternalOutput")

    ag_xn_in = nc.dram_tensor("ag_xn_in", [C, TOWN], BF16)
    ag_xn_out = nc.dram_tensor("ag_xn_out", [NCORES * C, TOWN], BF16)
    a2a_y_in = nc.dram_tensor("a2a_y_in", [NCORES * HLOC * 128, TOWN], BF16)
    a2a_y_out = nc.dram_tensor("a2a_y_out", [C, TOWN], BF16)
    qT_sp = nc.dram_tensor("qT_sp", [128, HLOC * T], BF16)
    kT_sp = nc.dram_tensor("kT_sp", [128, HLOC * T], BF16)
    v_sp = nc.dram_tensor("v_sp", [HLOC * T, 128], BF16)
    ffn_dram = nc.dram_tensor("ffn_dram", [C, TOWN], F32)

    dbg_t = {}
    if dbg:
        dbg_t["dbg_xn"] = nc.dram_tensor("dbg_xn", [C, TOWN], BF16, kind="ExternalOutput")
        dbg_t["dbg_q"] = nc.dram_tensor("dbg_q", [128, HLOC * T], BF16, kind="ExternalOutput")
        dbg_t["dbg_k"] = nc.dram_tensor("dbg_k", [128, HLOC * T], BF16, kind="ExternalOutput")
        dbg_t["dbg_v"] = nc.dram_tensor("dbg_v", [HLOC * T, 128], BF16, kind="ExternalOutput")
        dbg_t["dbg_y"] = nc.dram_tensor("dbg_y", [C, TOWN], BF16, kind="ExternalOutput")
        dbg_t["dbg_ffn"] = nc.dram_tensor("dbg_ffn", [C, TOWN], F32, kind="ExternalOutput")
        dbg_t["dbg_akv"] = nc.dram_tensor("dbg_akv", [10, 1024], BF16, kind="ExternalOutput")
        dbg_t["dbg_b"] = nc.dram_tensor("dbg_b", [128, 8], F32, kind="ExternalOutput")
        dbg_t["dbg_ffn2"] = nc.dram_tensor("dbg_ffn2", [C, TOWN], F32, kind="ExternalOutput")
        dbg_t["dbg_nrm"] = nc.dram_tensor("dbg_nrm", [2, 8], F32, kind="ExternalOutput")
        dbg_t["dbg_akt"] = nc.dram_tensor("dbg_akt", [128, 40], BF16, kind="ExternalOutput")
        dbg_t["dbg_sxnd"] = nc.dram_tensor("dbg_sxnd", [128, NCC * TOWN], BF16,
                                           kind="ExternalOutput")

    cnt = {}

    with ExitStack() as top:
        e = top.enter_context
        sems = {}
        for nm in SEM_NAMES:
            sems[nm] = e(nc.semaphore(nm))

        ones_bf = e(nc.sbuf_tensor("ones_bf", [128, 1], BF16))
        ones_rf = e(nc.sbuf_tensor("ones_rf", [1, 128], F32))
        ones_cf = e(nc.sbuf_tensor("ones_cf", [128, 1], F32))
        zp_sb = e(nc.sbuf_tensor("zp_sb", [1, C], BF16))
        scale_sb = e(nc.sbuf_tensor("scale_sb", [128, 32], F32))
        bias_sb = e(nc.sbuf_tensor("bias_sb", [128, 32], F32))
        neg_xsum = e(nc.sbuf_tensor("neg_xsum", [1, TOWN], BF16))
        trimask = e(nc.sbuf_tensor("trimask", [128, 128], BF16))
        ident = e(nc.sbuf_tensor("ident", [128, 128], BF16))
        gate_sb = e(nc.sbuf_tensor("gate_sb", [1, 1], F32))
        sakv = e(nc.sbuf_tensor("sakv", [10, 1024], BF16))
        sakT = e(nc.sbuf_tensor("sakT", [128, HLOC * 10], BF16))
        qknorm = e(nc.sbuf_tensor("qknorm", [1, 8], F32))
        aknorm = e(nc.sbuf_tensor("aknorm", [1, 4], F32))
        bb8 = e(nc.sbuf_tensor("bb8", [128, 8], F32))
        epsc = e(nc.sbuf_tensor("epsc", [1, 1], F32))

        with ExitStack() as pha:
            ea = pha.enter_context
            xch2 = ea(nc.sbuf_tensor("xch2", [128, 4 * TOWN], F32))
            mu = ea(nc.sbuf_tensor("mu", [1, TOWN], F32))
            var = ea(nc.sbuf_tensor("var", [1, TOWN], F32))
            lnv = ea(nc.sbuf_tensor("lnv", [1, TOWN], F32))
            rstd = ea(nc.sbuf_tensor("rstd", [1, TOWN], F32))
            nmr = ea(nc.sbuf_tensor("nmr", [1, TOWN], F32))
            rb = ea(nc.sbuf_tensor("rb", [128, TOWN], F32))
            mb = ea(nc.sbuf_tensor("mb", [128, TOWN], F32))
            sxn = ea(nc.sbuf_tensor("sxn", [128, NCC * TOWN], BF16))

            with ExitStack() as a1:
                e1 = a1.enter_context
                sqch2 = e1(nc.sbuf_tensor("sqch2", [128, 2 * TOWN], F32))
                ps_ln = e1(nc.psum_tensor("ps_ln", [1, 4 * 512], F32))
                ps_bc = e1(nc.psum_tensor("ps_bc", [128, 1024], F32))

                S = Sched(cnt)
                for dst, src in ((trimask, trimask_in), (ident, ident_in),
                                 (scale_sb, scale_pc), (bias_sb, bias_pc),
                                 (gate_sb, gating_in)):
                    S.add("sync", lambda ee, d=dst, s=src: ee.dma_start(
                        out=d[:, :], in_=s[:, :]), inc="dC", n=16)
                S.add("gpsimd", lambda ee: ee.dma_start(out=zp_sb[:, :], in_=zp_row[:, :]),
                      inc="dC", n=16)
                S.add("vector", lambda ee: ee.memset(ones_bf[:, :], 1.0), inc="dve")
                S.add("vector", lambda ee: ee.memset(ones_rf[:, :], 1.0), inc="dve")
                S.add("vector", lambda ee: ee.memset(ones_cf[:, :], 1.0), inc="dve")
                S.add("vector", lambda ee: ee.memset(qknorm[:, :], 0.0), inc="dve")
                S.add("vector", lambda ee: ee.memset(aknorm[:, :], 0.0), inc="dve")
                S.add("vector", lambda ee: ee.memset(epsc[:, :], float(EPS)), inc="dve")
                memset_done = S.v("dve")

                pe_after = {}
                act_after = {}
                for cc in range(NCC):
                    b = cc % 2
                    dxb = "dx%d" % b
                    waits = []
                    if cc >= 2:
                        waits = [("pe", pe_after[cc - 2]), ("act", act_after[cc - 2])]
                    ld = S.add("sync",
                               lambda ee, cc=cc, b=b: ee.dma_start(
                                   out=xch2[:, b * TOWN:(b + 1) * TOWN],
                                   in_=xT_perm[128 * cc:128 * (cc + 1), :]),
                               waits=waits, inc=dxb, n=16)
                    aw = [(dxb, ld)]
                    if cc >= 2:
                        aw.append(("pe", pe_after[cc - 2]))
                    act_after[cc] = S.add(
                        "scalar",
                        lambda ee, b=b: ee.activation(
                            out=sqch2[:, b * TOWN:(b + 1) * TOWN],
                            in_=xch2[:, b * TOWN:(b + 1) * TOWN], func=AF.Square),
                        waits=aw, inc="act")
                    st = (cc == 0)
                    sp = (cc == NCC - 1)
                    for tt in range(2):
                        S.add("tensor",
                              lambda ee, b=b, tt=tt, st=st, sp=sp: ee.matmul(
                                  out=ps_ln[0:1, 512 * tt:512 * (tt + 1)],
                                  lhsT=ones_cf[:, 0:1],
                                  rhs=xch2[:, b * TOWN + 512 * tt: b * TOWN + 512 * (tt + 1)],
                                  start=st, stop=sp),
                              waits=[(dxb, ld)] if tt == 0 else ())
                    for tt in range(2):
                        w = [("act", act_after[cc])] if tt == 0 else ()
                        r = S.add(
                            "tensor",
                            lambda ee, b=b, tt=tt, st=st, sp=sp: ee.matmul(
                                out=ps_ln[0:1, 1024 + 512 * tt:1024 + 512 * (tt + 1)],
                                lhsT=ones_cf[:, 0:1],
                                rhs=sqch2[:, b * TOWN + 512 * tt: b * TOWN + 512 * (tt + 1)],
                                start=st, stop=sp),
                            waits=w, inc=("pe" if tt == 1 else None))
                        if r is not None:
                            pe_after[cc] = r
                ln_mms_done = S.v("pe")

                for tt in range(2):
                    S.add("vector",
                          lambda ee, tt=tt: ee.tensor_scalar(
                              out=mu[0:1, 512 * tt:512 * (tt + 1)],
                              in0=ps_ln[0:1, 512 * tt:512 * (tt + 1)],
                              scalar1=1.0 / C, scalar2=None, op0=ALU.mult),
                          waits=[("pe", ln_mms_done)] if tt == 0 else (), inc="dve")
                    S.add("vector",
                          lambda ee, tt=tt: ee.tensor_scalar(
                              out=var[0:1, 512 * tt:512 * (tt + 1)],
                              in0=ps_ln[0:1, 1024 + 512 * tt:1024 + 512 * (tt + 1)],
                              scalar1=1.0 / C, scalar2=None, op0=ALU.mult),
                          inc="dve")
                S.add("vector", lambda ee: ee.tensor_tensor(
                    out=lnv[0:1, :], in0=mu[0:1, :], in1=mu[0:1, :], op=ALU.mult), inc="dve")
                S.add("vector", lambda ee: ee.tensor_tensor(
                    out=var[0:1, :], in0=var[0:1, :], in1=lnv[0:1, :], op=ALU.subtract),
                    inc="dve")
                stats_dve = S.v("dve")
                S.add("scalar", lambda ee: ee.activation(
                    out=lnv[0:1, :], in_=var[0:1, :], func=AF.Ln, bias=epsc[0:1, 0:1]),
                    waits=[("dve", stats_dve)], inc="act")
                S.add("scalar", lambda ee: ee.activation(
                    out=rstd[0:1, :], in_=lnv[0:1, :], func=AF.Exp, scale=-0.5), inc="act")
                rstd_act = S.v("act")
                S.add("vector", lambda ee: ee.scalar_tensor_tensor(
                    out=nmr[0:1, :], in0=mu[0:1, :], scalar=-1.0, in1=rstd[0:1, :],
                    op0=ALU.mult, op1=ALU.mult),
                    waits=[("act", rstd_act)], inc="dve")
                nmr_dve = S.v("dve")

                for tt in range(2):
                    S.add("tensor", lambda ee, tt=tt: ee.matmul(
                        out=ps_bc[:, 512 * tt:512 * (tt + 1)], lhsT=ones_rf[0:1, :],
                        rhs=rstd[0:1, 512 * tt:512 * (tt + 1)], start=True, stop=True),
                        waits=[("act", rstd_act), ("dve", memset_done)] if tt == 0 else (),
                        inc="pe")
                rb_mm = S.v("pe")
                S.add("vector", lambda ee: ee.tensor_copy(out=rb[:, :], in_=ps_bc[:, :]),
                      waits=[("pe", rb_mm)], inc="dve")
                rb_done = S.v("dve")
                for tt in range(2):
                    S.add("tensor", lambda ee, tt=tt: ee.matmul(
                        out=ps_bc[:, 512 * tt:512 * (tt + 1)], lhsT=ones_rf[0:1, :],
                        rhs=nmr[0:1, 512 * tt:512 * (tt + 1)], start=True, stop=True),
                        waits=[("dve", max(nmr_dve, rb_done))] if tt == 0 else (), inc="pe")
                mb_mm = S.v("pe")
                S.add("vector", lambda ee: ee.tensor_copy(out=mb[:, :], in_=ps_bc[:, :]),
                      waits=[("pe", mb_mm)], inc="dve")

                emit_block(nc, sems, S, "A1")

            with ExitStack() as a2:
                e2 = a2.enter_context
                xnt = e2(nc.sbuf_tensor("xnt", [128, TOWN], F32))
                swd = e2(nc.sbuf_tensor("swd", [128, 2 * 2048], I32))
                silo = e2(nc.sbuf_tensor("silo", [128, 2048], I32))
                sihi = e2(nc.sbuf_tensor("sihi", [128, 2048], I32))
                wnib = e2(nc.sbuf_tensor("wnib", [128, 2 * C], BF16))
                fbuf = e2(nc.sbuf_tensor("fbuf", [128, 4 * 512], F32))
                sad = e2(nc.sbuf_tensor("sad", [128, NCC * 10], BF16))
                swbig = e2(nc.sbuf_tensor("swbig", [128, 16 * 512], BF16))
                sqa = e2(nc.sbuf_tensor("sqa", [128, 40], BF16))
                ps_xs = e2(nc.psum_tensor("ps_xs", [1, 1024], F32))
                ps_ak = e2(nc.psum_tensor("ps_ak", [10, 1024], F32))
                ps_tr = e2(nc.psum_tensor("ps_tr", [128, 16], BF16))
                ps_nm = e2(nc.psum_tensor("ps_nm", [1, 64], F32))
                ps_f = e2(nc.psum_tensor("ps_f", [128, 2 * 512], F32))

                S = Sched(cnt)
                adl_t = S.add("gpsimd", lambda ee: ee.dma_start(
                    out=sad[:, :], in_=adT[:, :]), inc="dC", n=16)
                dC_A2 = S.v("dC")

                dve_p2 = {}
                for cc in range(NCC):
                    b = cc % 4
                    dxb = "dx%d" % b
                    waits = []
                    if cc >= 4:
                        waits = [("dve", dve_p2[cc - 4])]
                    ld = S.add("sync",
                               lambda ee, cc=cc, b=b: ee.dma_start(
                                   out=xch2[:, b * TOWN:(b + 1) * TOWN],
                                   in_=xT_perm[128 * cc:128 * (cc + 1), :]),
                               waits=waits, inc=dxb, n=16)
                    S.add("vector", lambda ee, b=b: ee.tensor_tensor(
                        out=xnt[:, :], in0=xch2[:, b * TOWN:(b + 1) * TOWN],
                        in1=rb[:, :], op=ALU.mult),
                        waits=[(dxb, ld)])
                    dve_p2[cc] = S.add("vector", lambda ee, cc=cc: ee.tensor_tensor(
                        out=sxn[:, TOWN * cc:TOWN * (cc + 1)], in0=xnt[:, :],
                        in1=mb[:, :], op=ALU.add), inc="dve")
                    S.add("scalar", lambda ee, cc=cc: ee.dma_start(
                        out=ag_xn_in[128 * cc:128 * (cc + 1), :],
                        in_=sxn[:, TOWN * cc:TOWN * (cc + 1)]),
                        waits=[("dve", dve_p2[cc])], inc="dsp", n=16)
                spill_done = S.v("dsp")

                S.add("gpsimd", lambda ee: ee.collective_compute(
                    "AllGather", ALU.bypass,
                    replica_groups=[list(range(NCORES))],
                    ins=[ag_xn_in.ap().opt()],
                    outs=[ag_xn_out.ap().opt()]),
                    waits=[("dsp", spill_done)], inc="cc")

                for cc in range(NCC):
                    for tt in range(2):
                        S.add("tensor", lambda ee, cc=cc, tt=tt: ee.matmul(
                            out=ps_xs[0:1, 512 * tt:512 * (tt + 1)], lhsT=ones_bf[:, 0:1],
                            rhs=sxn[:, TOWN * cc + 512 * tt:TOWN * cc + 512 * (tt + 1)],
                            start=(cc == 0), stop=(cc == NCC - 1)),
                            waits=[("dve", dve_p2[cc]), ("dve", memset_done)]
                            if tt == 0 else (),
                            inc=("pe" if (cc == NCC - 1 and tt == 1) else None))
                xs_mm = S.v("pe")
                S.add("vector", lambda ee: ee.tensor_scalar(
                    out=neg_xsum[0:1, :], in0=ps_xs[0:1, :], scalar1=-1.0, scalar2=None,
                    op0=ALU.mult), waits=[("pe", xs_mm)], inc="dve")
                negxs_done = S.v("dve")

                def ffn_chunk(S, oc, slot, sxn_src, swd, silo, sihi, wnib, fbuf, ps_f,
                              key, waits_extra=()):
                    b = slot % 2
                    dwb = "dw%d" % b
                    w0 = list(waits_extra)
                    prev_dve = S.cnt.get("%s_dve_%d" % (key, slot - 2), 0)
                    if prev_dve:
                        w0.append(("dve", prev_dve))
                    wld = S.add("gpsimd", lambda ee, oc=oc, b=b: ee.dma_start(
                        out=swd.ap()[:, 2048 * b:2048 * (b + 1)].rearrange(
                            "p (a o) -> p a o", a=16),
                        in_=wffn_pk.ap()[2048 * oc:2048 * (oc + 1), :].rearrange(
                            "(a p) o -> p a o", p=128)),
                        waits=w0, inc=dwb, n=16)
                    lo = S.add("vector", lambda ee, b=b: ee.tensor_scalar(
                        out=silo[:, :], in0=swd[:, 2048 * b:2048 * (b + 1)],
                        scalar1=15, scalar2=None, op0=ALU.bitwise_and),
                        waits=[(dwb, wld)], inc="dve")
                    hi = S.add("vector", lambda ee, b=b: ee.tensor_scalar(
                        out=sihi[:, :], in0=swd[:, 2048 * b:2048 * (b + 1)],
                        scalar1=4, scalar2=15, op0=ALU.logical_shift_right,
                        op1=ALU.bitwise_and), inc="dve")
                    pw = [("pe", S.cnt.get("%s_pe_%d" % (key, slot - 2), 0))]
                    S.add("vector", lambda ee, b=b: ee.tensor_copy(
                        out=wnib[:, C * b:C * b + 2048], in_=silo[:, :]),
                        waits=pw, inc="dve")
                    ch = S.add("vector", lambda ee, b=b: ee.tensor_copy(
                        out=wnib[:, C * b + 2048:C * b + C], in_=sihi[:, :]),
                        inc="dve")
                    last_pe = 0
                    for tt in range(2):
                        prev_act = S.cnt.get("%s_actt_%d_%d" % (key, slot - 1, tt), 0)
                        for cc2 in range(NCC):
                            w = []
                            if cc2 == 0:
                                w = [("dve", ch)]
                                w.append(("act", prev_act))
                            S.add("tensor", lambda ee, b=b, tt=tt, cc2=cc2: ee.matmul(
                                out=ps_f[:, 512 * tt:512 * (tt + 1)],
                                lhsT=wnib[:, C * b + 128 * cc2:C * b + 128 * (cc2 + 1)],
                                rhs=sxn_src[:, TOWN * cc2 + 512 * tt:
                                            TOWN * cc2 + 512 * (tt + 1)],
                                start=(cc2 == 0), stop=False), waits=w)
                        last_pe = S.add("tensor", lambda ee, oc=oc, tt=tt: ee.matmul(
                            out=ps_f[:, 512 * tt:512 * (tt + 1)],
                            lhsT=zp_sb[0:1, 128 * oc:128 * (oc + 1)],
                            rhs=neg_xsum[0:1, 512 * tt:512 * (tt + 1)],
                            start=False, stop=True), inc="pe")
                        fr = 2 * b + tt
                        dof = "do%d" % fr
                        av = S.add("scalar", lambda ee, oc=oc, fr=fr, tt=tt: ee.activation(
                            out=fbuf[:, 512 * fr:512 * (fr + 1)],
                            in_=ps_f[:, 512 * tt:512 * (tt + 1)], func=AF.Identity,
                            scale=scale_sb[:, oc:oc + 1], bias=bias_sb[:, oc:oc + 1]),
                            waits=[("pe", last_pe),
                                   (dof, S.cnt.get("%s_dma_%d_%d" % (key, slot - 2, tt), 0))],
                            inc="act")
                        S.cnt["%s_actt_%d_%d" % (key, slot, tt)] = av
                        S.cnt["%s_dma_%d_%d" % (key, slot, tt)] = S.add(
                            "sync", lambda ee, oc=oc, fr=fr, tt=tt: ee.dma_start(
                                out=ffn_dram[128 * oc:128 * (oc + 1),
                                             512 * tt:512 * (tt + 1)],
                                in_=fbuf[:, 512 * fr:512 * (fr + 1)]),
                            waits=[("act", av)], inc=dof, n=16)
                    S.cnt["%s_pe_%d" % (key, slot)] = last_pe
                    S.cnt["%s_dve_%d" % (key, slot)] = hi

                for oc in range(16):
                    extra = []
                    if oc == 0:
                        extra = [("dve", max(negxs_done, dve_p2[NCC - 1])),
                                 ("dC", dC_A2)]
                    ffn_chunk(S, oc, oc, sxn, swd, silo, sihi, wnib, fbuf, ps_f,
                              "_f1", waits_extra=extra)

                ak_mm = {}
                for ot in range(2):
                    srct = wakT if ot == 0 else wvT
                    for h in range(2):
                        waits = []
                        if (ot, h) != (0, 0):
                            waits = [("pe", ak_mm[(ot, 16 * h - 1) if h == 1 else
                                                  (0, NCC - 1)])]
                        wl = S.add("gpsimd", lambda ee, srct=srct, h=h: ee.dma_start(
                            out=swbig.ap().rearrange("p (a o) -> p a o", a=16),
                            in_=srct.ap()[2048 * h:2048 * (h + 1), :].rearrange(
                                "(a p) o -> p a o", p=128)),
                            waits=waits, inc="da0", n=16)
                        for cc in range(16 * h, 16 * (h + 1)):
                            mw = []
                            if cc == 16 * h:
                                mw = [("da0", wl), ("dC", dC_A2)]
                            ak_mm[(ot, cc)] = S.add(
                                "tensor", lambda ee, ot=ot, cc=cc, h=h: ee.matmul(
                                    out=ps_ak[0:10, 512 * ot:512 * (ot + 1)],
                                    lhsT=sad[:, 10 * cc:10 * (cc + 1)],
                                    rhs=swbig[:, 512 * (cc - 16 * h):
                                              512 * (cc - 16 * h + 1)],
                                    start=(cc == 0), stop=(cc == NCC - 1)),
                                waits=mw, inc="pe")
                S.add("vector", lambda ee: ee.tensor_copy(out=sakv[:, :], in_=ps_ak[0:10, :]),
                      waits=[("pe", ak_mm[(1, NCC - 1)])], inc="dve")
                sakv_done = S.v("dve")

                for hl in range(HLOC):
                    tr = S.add("tensor", lambda ee, hl=hl: ee.transpose(
                        out=ps_tr[:, 0:10], in_=sakv[0:10, 128 * hl:128 * (hl + 1)],
                        identity=ident[0:10, 0:10]),
                        waits=[("dve", S.v("dve")), ("dC", dC_A2)], inc="pe")
                    cpd = S.add("vector", lambda ee, hl=hl: ee.tensor_copy(
                        out=sakT[:, 10 * hl:10 * (hl + 1)], in_=ps_tr[:, 0:10]),
                        waits=[("pe", tr)], inc="dve")
                    S.add("scalar", lambda ee, hl=hl: ee.activation(
                        out=sqa[:, 10 * hl:10 * (hl + 1)],
                        in_=sakT[:, 10 * hl:10 * (hl + 1)],
                        func=AF.Square),
                        waits=[("dve", cpd)], inc="act")
                nm_last = S.add("tensor", lambda ee: ee.matmul(
                    out=ps_nm[0:1, 0:40], lhsT=ones_bf[:, 0:1],
                    rhs=sqa[:, 0:40],
                    start=True, stop=True), waits=[("act", S.v("act"))], inc="pe")
                for hl in range(HLOC):
                    S.add("vector", lambda ee, hl=hl: ee.reduce_max(
                        out=aknorm[0:1, hl:hl + 1], in_=ps_nm[0:1, 10 * hl:10 * (hl + 1)],
                        axis=AX.X),
                        waits=[("pe", nm_last)] if hl == 0 else (), inc="dve")


                emit_block(nc, sems, S, "A2")

        with ExitStack() as phb:
            eb = phb.enter_context
            cosT = eb(nc.sbuf_tensor("cosT", [128, T], BF16))
            sinT = eb(nc.sbuf_tensor("sinT", [128, T], BF16))
            sxt = eb(nc.sbuf_tensor("sxt", [128, 2 * NCC * 512], BF16))
            scr1 = eb(nc.sbuf_tensor("scr1", [128, 512], F32))
            scr2 = eb(nc.sbuf_tensor("scr2", [128, 512], F32))
            rbuf = eb(nc.sbuf_tensor("rbuf", [128, 4 * 512], BF16))

            with ExitStack() as b1:
                e1 = b1.enter_context
                swqk = e1(nc.sbuf_tensor("swqk", [128, 8 * C], BF16))
                ps_qk = e1(nc.psum_tensor("ps_qk", [128, 8 * 512], F32))

                S = Sched(cnt)
                S.add("sync", lambda ee: ee.dma_start(out=cosT[:, :], in_=cosT_in[:, :]),
                      inc="dC", n=16)
                S.add("sync", lambda ee: ee.dma_start(
                    out=sinT[:, :], in_=sinT_in[:, :]), inc="dC", n=16)
                dC_B1 = S.v("dC")
                for b in range(8):
                    S.add("gpsimd", lambda ee, b=b: ee.dma_start(
                        out=swqk.ap()[:, C * b:C * (b + 1)].rearrange(
                            "p (a o) -> p a o", a=NCC),
                        in_=watt_qkT.ap()[C * b:C * (b + 1), :].rearrange(
                            "(a p) o -> p a o", p=128)), inc="dWq", n=16)
                dWq_all = S.v("dWq")
                rope_done = {}
                for j in range(16):
                    b = j % 2
                    dtb = "dt%d" % b
                    waits = [("cc", 1)] if j == 0 else []
                    if j >= 2:
                        waits.append(("pe", S.cnt["_b1_pe_%d" % (j - 2)]))
                    ld = S.add("sync", lambda ee, j=j, b=b: ee.dma_start(
                        out=sxt.ap()[:, NCC * 512 * b:NCC * 512 * (b + 1)].rearrange(
                            "p (a t) -> p a t", a=NCC),
                        in_=ag_xn_out.ap()[C * (j // 2):C * (j // 2 + 1),
                                           512 * (j % 2):512 * (j % 2) + 512].rearrange(
                            "(a p) t -> p a t", p=128)),
                        waits=waits, inc=dtb, n=16)
                    for oc in range(8):
                        w = [(dtb, ld)]
                        if oc == 0 and j == 0:
                            w.append(("dWq", dWq_all))
                        if j >= 1:
                            w.append(("dve", rope_done[(j - 1, oc)]))
                        for cc in range(NCC):
                            S.add("tensor", lambda ee, oc=oc, b=b, cc=cc: ee.matmul(
                                out=ps_qk[:, 512 * oc:512 * (oc + 1)],
                                lhsT=swqk[:, C * oc + 128 * cc:C * oc + 128 * (cc + 1)],
                                rhs=sxt[:, NCC * 512 * b + 512 * cc:
                                        NCC * 512 * b + 512 * (cc + 1)],
                                start=(cc == 0), stop=(cc == NCC - 1)),
                                waits=w if cc == 0 else (),
                                inc=("pe" if cc == NCC - 1 else None))
                        mmn = S.v("pe")
                        rb_ = (8 * j + oc) % 4
                        drb = "dr%d" % rb_
                        rw = [("pe", mmn), ("dC", dC_B1)]
                        prev_use = 8 * j + oc - 4
                        if prev_use >= 0:
                            rw.append((drb, S.cnt["_b1_spd_%d" % prev_use]))
                        S.add("vector", lambda ee, oc=oc, j=j: ee.tensor_tensor(
                            out=scr1[:, :], in0=ps_qk[:, 512 * oc:512 * (oc + 1)],
                            in1=cosT[:, 512 * j:512 * (j + 1)], op=ALU.mult),
                            waits=rw)
                        S.add("vector", lambda ee, oc=oc, j=j: ee.tensor_tensor(
                            out=scr2[0:64, :],
                            in0=ps_qk[64:128, 512 * oc:512 * (oc + 1)],
                            in1=sinT[0:64, 512 * j:512 * (j + 1)], op=ALU.mult))
                        S.add("vector", lambda ee, oc=oc, j=j: ee.tensor_tensor(
                            out=scr2[64:128, :],
                            in0=ps_qk[0:64, 512 * oc:512 * (oc + 1)],
                            in1=sinT[64:128, 512 * j:512 * (j + 1)], op=ALU.mult))
                        rope_done[(j, oc)] = S.add(
                            "vector", lambda ee, rb_=rb_: ee.tensor_tensor(
                                out=rbuf[:, 512 * rb_:512 * (rb_ + 1)], in0=scr1[:, :],
                                in1=scr2[:, :], op=ALU.add), inc="dve")
                        dst = qT_sp if oc < 4 else kT_sp
                        hoff = (oc % 4) * T + 512 * j
                        S.cnt["_b1_spd_%d" % (8 * j + oc)] = S.add(
                            "gpsimd", lambda ee, dst=dst, hoff=hoff, rb_=rb_: ee.dma_start(
                                out=dst[:, hoff:hoff + 512],
                                in_=rbuf[:, 512 * rb_:512 * (rb_ + 1)]),
                            waits=[("dve", rope_done[(j, oc)])], inc=drb, n=16)
                    S.cnt["_b1_pe_%d" % j] = S.v("pe")
                emit_block(nc, sems, S, "B1")

            with ExitStack() as b2:
                e2 = b2.enter_context
                swv = e2(nc.sbuf_tensor("swv", [128, NCC * 512], BF16))
                vbuf = e2(nc.sbuf_tensor("vbuf", [128, 2 * 512], BF16))
                ps_v = e2(nc.psum_tensor("ps_v", [128, 2 * 512], F32))

                S = Sched(cnt)
                S.add("gpsimd", lambda ee: ee.dma_start(
                    out=swv.ap().rearrange("p (a o) -> p a o", a=NCC),
                    in_=wvT.ap().rearrange("(a p) o -> p a o", p=128)), inc="dWq", n=16)
                dWq_b2 = S.v("dWq")
                for j in range(16):
                    b = j % 2
                    dtb = "dt%d" % b
                    waits = []
                    if j >= 2:
                        waits.append(("pe", S.cnt["_b2_pe_%d" % (j - 2)]))
                    ld = S.add("sync", lambda ee, j=j, b=b: ee.dma_start(
                        out=sxt.ap()[:, NCC * 512 * b:NCC * 512 * (b + 1)].rearrange(
                            "p (a t) -> p a t", a=NCC),
                        in_=ag_xn_out.ap()[C * (j // 2):C * (j // 2 + 1),
                                           512 * (j % 2):512 * (j % 2) + 512].rearrange(
                            "(a p) t -> p a t", p=128)),
                        waits=waits, inc=dtb, n=16)
                    for s in range(4):
                        it = 4 * j + s
                        sb_ = it % 2
                        dvb = "dv%d" % sb_
                        w = [(dtb, ld)]
                        if it == 0:
                            w.append(("dWq", dWq_b2))
                        if it >= 2:
                            w.append(("act", S.cnt["_b2_ev_%d" % (it - 2)]))
                        for cc in range(NCC):
                            S.add("tensor", lambda ee, b=b, s=s, cc=cc, sb_=sb_: ee.matmul(
                                out=ps_v[:, 512 * sb_:512 * (sb_ + 1)],
                                lhsT=sxt[:, NCC * 512 * b + 512 * cc + 128 * s:
                                         NCC * 512 * b + 512 * cc + 128 * (s + 1)],
                                rhs=swv[:, 512 * cc:512 * (cc + 1)],
                                start=(cc == 0), stop=(cc == NCC - 1)),
                                waits=w if cc == 0 else (),
                                inc=("pe" if cc == NCC - 1 else None))
                        mmn = S.v("pe")
                        evw = [("pe", mmn)]
                        if it >= 2:
                            evw.append((dvb, S.cnt["_b2_sp_%d" % (it - 2)]))
                        ev = S.add("scalar", lambda ee, sb_=sb_: ee.copy(
                            out=vbuf[:, 512 * sb_:512 * (sb_ + 1)],
                            in_=ps_v[:, 512 * sb_:512 * (sb_ + 1)]),
                            waits=evw, inc="act")
                        S.cnt["_b2_ev_%d" % it] = ev
                        for hl in range(HLOC):
                            S.add("gpsimd", lambda ee, hl=hl, it=it, sb_=sb_: ee.dma_start(
                                out=v_sp[T * hl + 128 * it:T * hl + 128 * (it + 1), :],
                                in_=vbuf[:, 512 * sb_ + 128 * hl:512 * sb_ + 128 * (hl + 1)]),
                                waits=[("act", ev)] if hl == 0 else (), inc=dvb, n=16)
                        S.cnt["_b2_sp_%d" % it] = S.v(dvb)
                    S.cnt["_b2_pe_%d" % j] = S.v("pe")
                emit_block(nc, sems, S, "B2")

        with ExitStack() as phc:
            ec = phc.enter_context
            kh = ec(nc.sbuf_tensor("kh", [128, 2 * T], BF16))
            vh = ec(nc.sbuf_tensor("vh", [128, 2 * T], BF16))
            qh = ec(nc.sbuf_tensor("qh", [128, 2 * T], BF16))
            pg = ec(nc.sbuf_tensor("pg", [128, 2 * 1024], BF16))
            pa = ec(nc.sbuf_tensor("pa", [10, 512], BF16))
            sdm = ec(nc.sbuf_tensor("sdm", [1, 512], F32))
            sdag = ec(nc.sbuf_tensor("sdag", [1, 512], F32))
            sbd = ec(nc.sbuf_tensor("sbd", [128, 512], F32))
            sbda = ec(nc.sbuf_tensor("sbda", [128, 512], F32))
            yt1 = ec(nc.sbuf_tensor("yt1", [128, 512], F32))
            yt2 = ec(nc.sbuf_tensor("yt2", [128, 512], F32))
            ybuf = ec(nc.sbuf_tensor("ybuf", [128, 2 * 512], BF16))
            nbuf = ec(nc.sbuf_tensor("nbuf", [128, 2 * 2048], BF16))
            nsq = ec(nc.sbuf_tensor("nsq", [128, 2048], BF16))
            bt8 = ec(nc.sbuf_tensor("bt8", [1, 8], F32))
            nseg = ec(nc.sbuf_tensor("nseg", [1, 32], F32))
            tm11 = ec(nc.sbuf_tensor("tm11", [1, 1], F32))
            ps_s = ec(nc.psum_tensor("ps_s", [128, 2 * 1024], F32))
            ps_y = ec(nc.psum_tensor("ps_y", [128, 512], F32))
            ps_ya = ec(nc.psum_tensor("ps_ya", [128, 512], F32))
            ps_d = ec(nc.psum_tensor("ps_d", [1, 512], F32))
            ps_da = ec(nc.psum_tensor("ps_da", [1, 512], F32))

            S = Sched(cnt)
            spill_fin = [("dr0", S.v("dr0")), ("dr1", S.v("dr1")),
                         ("dr2", S.v("dr2")), ("dr3", S.v("dr3"))]
            vsp_fin = [("dv0", S.v("dv0")), ("dv1", S.v("dv1"))]
            for idx, (src, base) in enumerate([(qT_sp, 0), (kT_sp, 4)]):
                for hl in range(HLOC):
                    for seg in range(4):
                        it = idx * 16 + hl * 4 + seg
                        b = it % 2
                        dnb = "dn%d" % b
                        w = list(spill_fin) if it == 0 else []
                        if it >= 2:
                            w.append(("act", S.cnt["_c_nsq_%d" % (it - 2)]))
                        ld = S.add("sync", lambda ee, src=src, hl=hl, seg=seg, b=b:
                                   ee.dma_start(
                                       out=nbuf[:, 2048 * b:2048 * (b + 1)],
                                       in_=src[:, T * hl + 2048 * seg:
                                               T * hl + 2048 * (seg + 1)]),
                                   waits=w, inc=dnb, n=16)
                        sqw = [(dnb, ld)]
                        if it >= 1:
                            sqw.append(("pe", S.cnt["_c_nmm_%d" % (it - 1)]))
                        S.cnt["_c_nsq_%d" % it] = S.add(
                            "scalar", lambda ee, b=b: ee.activation(
                                out=nsq[:, :], in_=nbuf[:, 2048 * b:2048 * (b + 1)],
                                func=AF.Square),
                            waits=sqw, inc="act")
                        mmw = [("act", S.cnt["_c_nsq_%d" % it])]
                        if it >= 1:
                            mmw.append(("dve", S.cnt["_c_nmx_%d" % (it - 1)]))
                        for k4 in range(4):
                            S.add("tensor", lambda ee, k4=k4: ee.matmul(
                                out=ps_s[0:1, 512 * k4:512 * (k4 + 1)],
                                lhsT=ones_bf[:, 0:1], rhs=nsq[:, 512 * k4:512 * (k4 + 1)],
                                start=True, stop=True),
                                waits=mmw if k4 == 0 else (),
                                inc=("pe" if k4 == 3 else None))
                        S.cnt["_c_nmm_%d" % it] = S.v("pe")
                        S.cnt["_c_nmx_%d" % it] = S.add(
                            "vector", lambda ee, it=it: ee.reduce_max(
                                out=nseg[0:1, it:it + 1], in_=ps_s[0:1, 0:2048],
                                axis=AX.X),
                            waits=[("pe", S.cnt["_c_nmm_%d" % it])], inc="dve")
            for oc8 in range(8):
                S.add("vector", lambda ee, oc8=oc8: ee.reduce_max(
                    out=qknorm[0:1, oc8:oc8 + 1], in_=nseg[0:1, 4 * oc8:4 * (oc8 + 1)],
                    axis=AX.X), inc="dve")
            S.add("vector", lambda ee: ee.engine_nop(), inc="dve")
            S.add("vector", lambda ee: ee.engine_nop(), inc="dve")
            norm_dve = S.v("dve")

            S.add("vector", lambda ee: ee.tensor_tensor(
                out=bt8[0:1, 0:4], in0=qknorm[0:1, 0:4], in1=qknorm[0:1, 4:8],
                op=ALU.mult), waits=[("dve", norm_dve)], inc="dve")
            S.add("vector", lambda ee: ee.tensor_tensor(
                out=bt8[0:1, 4:8], in0=qknorm[0:1, 0:4], in1=aknorm[0:1, 0:4],
                op=ALU.mult), inc="dve")
            b_dve = S.v("dve")
            S.add("scalar", lambda ee: ee.activation(
                out=bt8[0:1, :], in_=bt8[0:1, :], func=AF.Ln, bias=epsc[0:1, 0:1]),
                waits=[("dve", b_dve)], inc="act")
            S.add("scalar", lambda ee: ee.activation(
                out=bt8[0:1, :], in_=bt8[0:1, :], func=AF.Exp, scale=0.5), inc="act")
            b_act = S.v("act")
            S.add("vector", lambda ee: ee.tensor_scalar(
                out=bt8[0:1, 0:4], in0=bt8[0:1, 0:4], scalar1=-1.02 * SCALE,
                scalar2=-0.1, op0=ALU.mult, op1=ALU.add),
                waits=[("act", b_act)], inc="dve")
            S.add("vector", lambda ee: ee.tensor_scalar(
                out=bt8[0:1, 4:8], in0=bt8[0:1, 4:8], scalar1=-1.02 * ASCALE,
                scalar2=-0.1, op0=ALU.mult, op1=ALU.add), inc="dve")
            nb_dve = S.v("dve")
            bmm = S.add("tensor", lambda ee: ee.matmul(
                out=ps_y[:, 0:8], lhsT=ones_rf[0:1, :], rhs=bt8[0:1, :],
                start=True, stop=True), waits=[("dve", nb_dve)], inc="pe")
            S.add("vector", lambda ee: ee.tensor_copy(out=bb8[:, :], in_=ps_y[:, 0:8]),
                  waits=[("pe", bmm)], inc="dve")
            bb_done = S.v("dve")

            for hl in range(HLOC):
                b = hl % 2
                dhb = "dh%d" % b
                w = [("dve", bb_done)]
                if hl == 0:
                    w += vsp_fin
                if hl >= 2:
                    w.append(("pe", S.cnt["_c_hpe_%d" % (hl - 2)]))
                S.add("sync", lambda ee, hl=hl, b=b: ee.dma_start(
                    out=qh[:, T * b:T * (b + 1)], in_=qT_sp[:, T * hl:T * (hl + 1)]),
                    waits=w, inc=dhb, n=16)
                S.add("sync", lambda ee, hl=hl, b=b: ee.dma_start(
                    out=kh[:, T * b:T * (b + 1)], in_=kT_sp[:, T * hl:T * (hl + 1)]),
                    inc=dhb, n=16)
                l3 = S.add("sync", lambda ee, hl=hl, b=b: ee.dma_start(
                    out=vh.ap()[:, T * b:T * (b + 1)].rearrange("p (a d) -> p a d", a=64),
                    in_=v_sp.ap()[T * hl:T * (hl + 1), :].rearrange(
                        "(a p) d -> p a d", p=128)),
                    inc=dhb, n=16)
                for qb in range(16):
                    qoff = T * b + 512 * qb
                    G = 2 * (qb + 1)
                    saw = [(dhb, l3), ("dve", S.cnt.get("_c_sbd", 0))]
                    sa = S.add("tensor", lambda ee, hl=hl, qoff=qoff: ee.matmul(
                        out=ps_s[0:10, 0:512], lhsT=sakT[:, 10 * hl:10 * (hl + 1)],
                        rhs=qh[:, qoff:qoff + 512], start=True, stop=True),
                        waits=saw, inc="pe")
                    expa = S.add("scalar", lambda ee, hl=hl: ee.activation(
                        out=pa[0:10, :], in_=ps_s[0:10, 0:512], func=AF.Exp,
                        bias=bb8[0:10, 4 + hl:5 + hl], scale=ASCALE),
                        waits=[("pe", sa)], inc="act")
                    dw_ = [("act", expa), ("dve", S.cnt.get("_c_yt2", 0))]
                    S.add("tensor", lambda ee: ee.matmul(
                        out=ps_da[0:1, :], lhsT=ones_bf[0:10, 0:1], rhs=pa[0:10, :],
                        start=True, stop=True), waits=dw_, inc="pe")
                    S.add("tensor", lambda ee, hl=hl: ee.matmul(
                        out=ps_ya[:, :],
                        lhsT=sakv[0:10, 512 + 128 * hl:512 + 128 * (hl + 1)],
                        rhs=pa[0:10, :], start=True, stop=True), inc="pe")
                    U = 4 * (qb + 1)
                    exp_cnt = {}
                    mask_cnt = {}

                    def emit_S(u):
                        sl = u % 4
                        sw = [("act", exp_cnt[u - 4] if u >= 4 else expa)]
                        smm = S.add("tensor", lambda ee, b=b, sl=sl, u=u,
                                    qoff=qoff: ee.matmul(
                                        out=ps_s[:, 512 * sl:512 * (sl + 1)],
                                        lhsT=kh[:, T * b + 128 * u:T * b + 128 * (u + 1)],
                                        rhs=qh[:, qoff:qoff + 512],
                                        start=True, stop=True),
                                    waits=sw, inc="pe")
                        exp_cnt[u] = S.add("scalar", lambda ee, sl=sl, hl=hl:
                                           ee.activation(
                                               out=pg[:, 512 * sl:512 * (sl + 1)],
                                               in_=ps_s[:, 512 * sl:512 * (sl + 1)],
                                               func=AF.Exp,
                                               bias=bb8[:, hl:hl + 1], scale=SCALE),
                                           waits=[("pe", smm)], inc="act")
                        if u >= 4 * qb:
                            sp_ = u - 4 * qb
                            mw_ = [("act", exp_cnt[u])]
                            if S.cnt.get("_c_mask1", 0) == 0:
                                mw_.append(("dC", S.v("dC")))
                                S.cnt["_c_mask1"] = 1
                            mask_cnt[u] = S.add(
                                "vector", lambda ee, sl=sl, sp_=sp_:
                                ee.tensor_tensor(
                                    out=pg[:, 512 * sl + 128 * sp_:
                                           512 * sl + 128 * (sp_ + 1)],
                                    in0=pg[:, 512 * sl + 128 * sp_:
                                           512 * sl + 128 * (sp_ + 1)],
                                    in1=trimask[:, :], op=ALU.mult),
                                waits=mw_, inc="dve")

                    def emit_AVD(u):
                        sl = u % 4
                        diag = (u >= 4 * qb)
                        avw = [("act", exp_cnt[u])]
                        if diag:
                            avw.append(("dve", mask_cnt[u]))
                        if u == 0:
                            avw.append(("dve", S.cnt.get("_c_yt1", 0)))
                            avw.append(("act", S.cnt.get("_c_lnd", 0)))
                        qlo = 128 * (u - 4 * qb) if diag else 0
                        S.add("tensor", lambda ee, b=b, sl=sl, u=u, qlo=qlo: ee.matmul(
                            out=ps_y[:, qlo:512],
                            lhsT=vh[:, T * b + 128 * u:T * b + 128 * (u + 1)],
                            rhs=pg[:, 512 * sl + qlo:512 * (sl + 1)],
                            start=(u == 0), stop=False),
                            waits=avw)
                        S.add("tensor", lambda ee, sl=sl, qlo=qlo, u=u: ee.matmul(
                            out=ps_d[0:1, qlo:512], lhsT=ones_bf[:, 0:1],
                            rhs=pg[:, 512 * sl + qlo:512 * (sl + 1)],
                            start=(u == 0), stop=False),
                            inc="pe")

                    for u0 in range(min(3, U)):
                        emit_S(u0)
                    for u in range(U):
                        emit_AVD(u)
                        if u + 3 < U:
                            emit_S(u + 3)
                    avd_done = S.v("pe")
                    S.add("scalar", lambda ee: ee.activation(
                        out=sdm[0:1, :], in_=ps_d[0:1, :], func=AF.Ln,
                        bias=0.0), waits=[("pe", avd_done)], inc="act")
                    S.add("scalar", lambda ee: ee.activation(
                        out=sdm[0:1, :], in_=sdm[0:1, :], func=AF.Exp,
                        scale=-1.0), inc="act")
                    S.add("scalar", lambda ee: ee.activation(
                        out=sdag[0:1, :], in_=ps_da[0:1, :], func=AF.Ln,
                        bias=0.0), inc="act")
                    S.add("scalar", lambda ee: ee.activation(
                        out=sdag[0:1, :], in_=sdag[0:1, :], func=AF.Exp,
                        scale=-1.0), inc="act")
                    rec_done = S.add("scalar", lambda ee: ee.activation(
                        out=sdag[0:1, :], in_=sdag[0:1, :], func=AF.Identity,
                        scale=gate_sb[0:1, 0:1]), inc="act")
                    S.cnt["_c_lnd"] = rec_done
                    bw = [("act", rec_done)]
                    S.add("tensor", lambda ee: ee.matmul(
                        out=ps_s[:, 0:512], lhsT=ones_rf[0:1, :], rhs=sdm[0:1, :],
                        start=True, stop=True), waits=bw, inc="pe")
                    bm2 = S.add("tensor", lambda ee: ee.matmul(
                        out=ps_s[:, 512:1024], lhsT=ones_rf[0:1, :], rhs=sdag[0:1, :],
                        start=True, stop=True), inc="pe")
                    S.add("vector", lambda ee: ee.tensor_copy(
                        out=sbd[:, :], in_=ps_s[:, 0:512]), waits=[("pe", bm2)], inc="dve")
                    S.cnt["_c_sbd"] = S.add("vector", lambda ee: ee.tensor_copy(
                        out=sbda[:, :], in_=ps_s[:, 512:1024]), inc="dve")
                    S.cnt["_c_yt1"] = S.add("vector", lambda ee: ee.tensor_tensor(
                        out=yt1[:, :], in0=ps_y[:, :], in1=sbd[:, :], op=ALU.mult),
                        inc="dve")
                    S.cnt["_c_yt2"] = S.add("vector", lambda ee: ee.tensor_tensor(
                        out=yt2[:, :], in0=ps_ya[:, :], in1=sbda[:, :], op=ALU.mult),
                        inc="dve")
                    yb = qb % 2
                    dyb = "dy%d" % yb
                    ydw = []
                    prevy = S.cnt.get("_c_ydma_%d_%d" % (hl, qb - 2), 0)
                    if prevy:
                        ydw.append((dyb, prevy))
                    yv = S.add("vector", lambda ee, yb=yb: ee.tensor_tensor(
                        out=ybuf[:, 512 * yb:512 * (yb + 1)], in0=yt1[:, :], in1=yt2[:, :],
                        op=ALU.add), waits=ydw, inc="dve")
                    S.cnt["_c_ydma_%d_%d" % (hl, qb)] = S.add(
                        "gpsimd", lambda ee, hl=hl, qb=qb, yb=yb: ee.dma_start(
                            out=a2a_y_in[512 * (qb // 2) + 128 * hl:
                                         512 * (qb // 2) + 128 * (hl + 1),
                                         512 * (qb % 2):512 * (qb % 2) + 512],
                            in_=ybuf[:, 512 * yb:512 * (yb + 1)]),
                        waits=[("dve", yv)], inc=dyb, n=16)
                S.cnt["_c_hpe_%d" % hl] = S.v("pe")

            def do_a2a(ee):
                with allow_ici_all_to_all():
                    return ee.collective_compute(
                        "AllToAll", ALU.bypass,
                        replica_groups=[list(range(NCORES))],
                        ins=[a2a_y_in.ap().opt()],
                        outs=[a2a_y_out.ap().opt()])
            S.add("gpsimd", do_a2a,
                  waits=[("dy0", S.v("dy0")), ("dy1", S.v("dy1"))], inc="cc")
            emit_block(nc, sems, S, "C")

        with ExitStack() as phd1:
            e1 = phd1.enter_context
            sxn_d = e1(nc.sbuf_tensor("sxn_d", [128, NCC * TOWN], BF16))
            swd2 = e1(nc.sbuf_tensor("swd2", [128, 2 * 2048], I32))
            silo2 = e1(nc.sbuf_tensor("silo2", [128, 2048], I32))
            sihi2 = e1(nc.sbuf_tensor("sihi2", [128, 2048], I32))
            wnib2 = e1(nc.sbuf_tensor("wnib2", [128, 2 * C], BF16))
            fbuf2 = e1(nc.sbuf_tensor("fbuf2", [128, 4 * 512], F32))
            ps_f2 = e1(nc.psum_tensor("ps_f2", [128, 2 * 512], F32))

            S = Sched(cnt)
            xl = S.add("sync", lambda ee: ee.dma_start(
                out=sxn_d.ap().rearrange("p (a t) -> p a t", a=NCC),
                in_=ag_xn_in.ap().rearrange("(a p) t -> p a t", p=128)),
                waits=[("dsp", S.v("dsp"))], inc="dxn", n=16)

            def ffn_chunk_d(S, oc, slot):
                b = slot % 2
                dwb = "dw%d" % b
                w0 = []
                prev_dve = S.cnt.get("_f2_dve_%d" % (slot - 2), 0)
                if prev_dve:
                    w0.append(("dve", prev_dve))
                wld = S.add("gpsimd", lambda ee, oc=oc, b=b: ee.dma_start(
                    out=swd2.ap()[:, 2048 * b:2048 * (b + 1)].rearrange(
                        "p (a o) -> p a o", a=16),
                    in_=wffn_pk.ap()[2048 * oc:2048 * (oc + 1), :].rearrange(
                        "(a p) o -> p a o", p=128)), waits=w0, inc=dwb, n=16)
                lo = S.add("vector", lambda ee, b=b: ee.tensor_scalar(
                    out=silo2[:, :], in0=swd2[:, 2048 * b:2048 * (b + 1)],
                    scalar1=15, scalar2=None, op0=ALU.bitwise_and),
                    waits=[(dwb, wld)], inc="dve")
                hi = S.add("vector", lambda ee, b=b: ee.tensor_scalar(
                    out=sihi2[:, :], in0=swd2[:, 2048 * b:2048 * (b + 1)],
                    scalar1=4, scalar2=15, op0=ALU.logical_shift_right,
                    op1=ALU.bitwise_and), inc="dve")
                pw = [("pe", S.cnt.get("_f2_pe_%d" % (slot - 2), 0))]
                S.add("vector", lambda ee, b=b: ee.tensor_copy(
                    out=wnib2[:, C * b:C * b + 2048], in_=silo2[:, :]), waits=pw,
                    inc="dve")
                ch = S.add("vector", lambda ee, b=b: ee.tensor_copy(
                    out=wnib2[:, C * b + 2048:C * b + C], in_=sihi2[:, :]),
                    inc="dve")
                last_pe = 0
                for tt in range(2):
                    prev_act = S.cnt.get("_f2_actt_%d_%d" % (slot - 1, tt), 0)
                    for cc2 in range(NCC):
                        w = []
                        if cc2 == 0:
                            w = [("dve", ch), ("act", prev_act), ("dxn", xl)]
                        S.add("tensor", lambda ee, b=b, tt=tt, cc2=cc2: ee.matmul(
                            out=ps_f2[:, 512 * tt:512 * (tt + 1)],
                            lhsT=wnib2[:, C * b + 128 * cc2:C * b + 128 * (cc2 + 1)],
                            rhs=sxn_d[:, TOWN * cc2 + 512 * tt:TOWN * cc2 + 512 * (tt + 1)],
                            start=(cc2 == 0), stop=False), waits=w)
                    last_pe = S.add("tensor", lambda ee, oc=oc, tt=tt: ee.matmul(
                        out=ps_f2[:, 512 * tt:512 * (tt + 1)],
                        lhsT=zp_sb[0:1, 128 * oc:128 * (oc + 1)],
                        rhs=neg_xsum[0:1, 512 * tt:512 * (tt + 1)],
                        start=False, stop=True), inc="pe")
                    fr = 2 * b + tt
                    dof = "do%d" % fr
                    av = S.add("scalar", lambda ee, oc=oc, fr=fr, tt=tt: ee.activation(
                        out=fbuf2[:, 512 * fr:512 * (fr + 1)],
                        in_=ps_f2[:, 512 * tt:512 * (tt + 1)], func=AF.Identity,
                        scale=scale_sb[:, oc:oc + 1], bias=bias_sb[:, oc:oc + 1]),
                        waits=[("pe", last_pe),
                               (dof, S.cnt.get("_f2_dma_%d_%d" % (slot - 2, tt), 0))],
                        inc="act")
                    S.cnt["_f2_actt_%d_%d" % (slot, tt)] = av
                    S.cnt["_f2_dma_%d_%d" % (slot, tt)] = S.add(
                        "sync", lambda ee, oc=oc, fr=fr, tt=tt: ee.dma_start(
                            out=ffn_dram[128 * oc:128 * (oc + 1), 512 * tt:512 * (tt + 1)],
                            in_=fbuf2[:, 512 * fr:512 * (fr + 1)]),
                        waits=[("act", av)], inc=dof, n=16)
                S.cnt["_f2_pe_%d" % slot] = last_pe
                S.cnt["_f2_dve_%d" % slot] = hi

            for i_, oc in enumerate(range(16, 32)):
                ffn_chunk_d(S, oc, i_)
            if dbg:
                S.add("gpsimd", lambda ee: ee.dma_start(
                    out=dbg_t["dbg_ffn2"][:, :], in_=ffn_dram[:, :]),
                    waits=[("do%d" % r, S.v("do%d" % r)) for r in range(4)],
                    inc="dD", n=16)
                S.add("gpsimd", lambda ee: ee.dma_start(
                    out=dbg_t["dbg_sxnd"][:, :], in_=sxn_d[:, :]), inc="dD", n=16)
                S.add("gpsimd", None, waits=[("dD", S.v("dD"))])
            emit_block(nc, sems, S, "D1")

        with ExitStack() as phd2:
            e2 = phd2.enter_context
            syt = e2(nc.sbuf_tensor("syt", [128, NCC * TOWN], BF16))
            swp = e2(nc.sbuf_tensor("swp", [128, 2 * C], BF16))
            sfn = e2(nc.sbuf_tensor("sfn", [128, 2 * TOWN], F32))
            sxr = e2(nc.sbuf_tensor("sxr", [128, 2 * TOWN], F32))
            pt1 = e2(nc.sbuf_tensor("pt1", [128, 512], F32))
            obuf = e2(nc.sbuf_tensor("obuf", [128, 2 * 512], F32))
            ps_p = e2(nc.psum_tensor("ps_p", [128, 2 * 512], F32))

            S = Sched(cnt)
            ffn_fin = [("do%d" % r, S.v("do%d" % r)) for r in range(4)]
            yl = S.add("sync", lambda ee: ee.dma_start(
                out=syt.ap().rearrange("p (a t) -> p a t", a=NCC),
                in_=a2a_y_out.ap().rearrange("(a p) t -> p a t", p=128)),
                waits=[("cc", 2)], inc="dyt", n=16)
            for oc in range(32):
                b = oc % 2
                dpb = "dp%d" % b
                dfb = "df%d" % b
                w = []
                if oc >= 2:
                    w.append(("pe", S.cnt["_d2_pe_%d" % (oc - 2)]))
                wl = S.add("gpsimd", lambda ee, oc=oc, b=b: ee.dma_start(
                    out=swp.ap()[:, C * b:C * (b + 1)].rearrange("p (a o) -> p a o",
                                                                 a=NCC),
                    in_=wprojT.ap()[C * oc:C * (oc + 1), :].rearrange(
                        "(a p) o -> p a o", p=128)), waits=w, inc=dpb, n=16)
                w2 = list(ffn_fin) if oc == 0 else []
                if oc >= 2:
                    w2.append(("dve", S.cnt["_d2_dve_%d" % (oc - 2)]))
                fl = S.add("sync", lambda ee, oc=oc, b=b: ee.dma_start(
                    out=sfn[:, TOWN * b:TOWN * (b + 1)],
                    in_=ffn_dram[128 * oc:128 * (oc + 1), :]), waits=w2, inc=dfb, n=16)
                fl = S.add("sync", lambda ee, oc=oc, b=b: ee.dma_start(
                    out=sxr[:, TOWN * b:TOWN * (b + 1)],
                    in_=xT_nat[128 * oc:128 * (oc + 1), :]), inc=dfb, n=16)
                for tt in range(2):
                    pw = [(dpb, wl), ("dyt", yl)]
                    prev_ov = S.cnt.get("_d2_ov_%d_%d" % (oc - 1, tt), 0)
                    if prev_ov:
                        pw.append(("dve", prev_ov))
                    for cc in range(NCC):
                        S.add("tensor", lambda ee, b=b, tt=tt, cc=cc: ee.matmul(
                            out=ps_p[:, 512 * tt:512 * (tt + 1)],
                            lhsT=swp[:, C * b + 128 * cc:C * b + 128 * (cc + 1)],
                            rhs=syt[:, TOWN * cc + 512 * tt:TOWN * cc + 512 * (tt + 1)],
                            start=(cc == 0), stop=(cc == NCC - 1)),
                            waits=pw if cc == 0 else (),
                            inc=("pe" if cc == NCC - 1 else None))
                    mmn = S.v("pe")
                    ew = [("pe", mmn), (dfb, fl)]
                    dzb = "dz%d" % tt
                    prev_od = S.cnt.get("_d2_odma_%d_%d" % (oc - 1, tt), 0)
                    if prev_od:
                        ew.append((dzb, prev_od))
                    S.add("vector", lambda ee, b=b, tt=tt: ee.tensor_tensor(
                        out=pt1[:, :], in0=ps_p[:, 512 * tt:512 * (tt + 1)],
                        in1=sfn[:, TOWN * b + 512 * tt:TOWN * b + 512 * (tt + 1)],
                        op=ALU.add), waits=ew)
                    ov = S.add("vector", lambda ee, b=b, tt=tt: ee.tensor_tensor(
                        out=obuf[:, 512 * tt:512 * (tt + 1)], in0=pt1[:, :],
                        in1=sxr[:, TOWN * b + 512 * tt:TOWN * b + 512 * (tt + 1)],
                        op=ALU.add), inc="dve")
                    S.cnt["_d2_ov_%d_%d" % (oc, tt)] = ov
                    S.cnt["_d2_odma_%d_%d" % (oc, tt)] = S.add(
                        "sync", lambda ee, oc=oc, tt=tt: ee.dma_start(
                            out=outT[128 * oc:128 * (oc + 1), 512 * tt:512 * (tt + 1)],
                            in_=obuf[:, 512 * tt:512 * (tt + 1)]),
                        waits=[("dve", ov)], inc=dzb, n=16)
                S.cnt["_d2_pe_%d" % oc] = S.v("pe")
                S.cnt["_d2_dve_%d" % oc] = S.v("dve")

            if dbg:
                alldone = [("dz0", S.v("dz0")), ("dz1", S.v("dz1"))] + \
                          [("do%d" % r, S.v("do%d" % r)) for r in range(4)] + [
                           ("dsp", S.v("dsp")), ("cc", 2),
                           ("dr0", S.v("dr0")), ("dr1", S.v("dr1")),
                           ("dr2", S.v("dr2")), ("dr3", S.v("dr3")),
                           ("dv0", S.v("dv0")), ("dv1", S.v("dv1"))]
                pairs = [(dbg_t["dbg_xn"], ag_xn_in), (dbg_t["dbg_q"], qT_sp),
                         (dbg_t["dbg_k"], kT_sp), (dbg_t["dbg_v"], v_sp),
                         (dbg_t["dbg_y"], a2a_y_out), (dbg_t["dbg_ffn"], ffn_dram)]
                first = True
                for dst, src in pairs:
                    S.add("gpsimd", lambda ee, dst=dst, src=src: ee.dma_start(
                        out=dst[:, :], in_=src[:, :]),
                        waits=alldone if first else (), inc="dD", n=16)
                    first = False
                S.add("gpsimd", lambda ee: ee.dma_start(
                    out=dbg_t["dbg_akv"][:, :], in_=sakv[:, :]), inc="dD", n=16)
                S.add("gpsimd", lambda ee: ee.dma_start(
                    out=dbg_t["dbg_b"][:, :], in_=bb8[:, :]), inc="dD", n=16)
                S.add("gpsimd", lambda ee: ee.dma_start(
                    out=dbg_t["dbg_nrm"][0:1, :], in_=qknorm[0:1, :]), inc="dD", n=16)
                S.add("gpsimd", lambda ee: ee.dma_start(
                    out=dbg_t["dbg_nrm"][1:2, 0:4], in_=aknorm[0:1, :]), inc="dD", n=16)
                S.add("gpsimd", lambda ee: ee.dma_start(
                    out=dbg_t["dbg_akt"][:, :], in_=sakT[:, :]), inc="dD", n=16)
                S.add("gpsimd", None, waits=[("dD", S.v("dD"))])

            S.add("sync", None, waits=[("dz0", S.v("dz0")), ("dz1", S.v("dz1"))])
            emit_block(nc, sems, S, "D2")

    return nc


def _stage(inputs):
    x = np.asarray(inputs["L0_attn_x"])
    watt = np.asarray(inputs["L0_attn_w_attn"])
    wproj = np.asarray(inputs["L0_attn_w_proj"])
    ad = np.asarray(inputs["L0_attn_adapter_wte"])
    gat = np.asarray(inputs["L0_attn_gating"])
    pk = np.asarray(inputs["L0_ffn_weight_packed"])
    sc = np.asarray(inputs["L0_ffn_scale"])
    zp = np.asarray(inputs["L0_ffn_zero_point"])
    bi = np.asarray(inputs["L0_ffn_bias"])

    permc = np.r_[np.arange(0, C, 2), np.arange(1, C, 2)]
    xT = np.ascontiguousarray(x[0].T.astype(np.float32))
    xTp = np.ascontiguousarray(xT[permc])

    wffn_st = np.ascontiguousarray(
        pk.reshape(C, 2048).T.reshape(2048, 32, 128).transpose(1, 0, 2)
    ).reshape(32 * 2048, 128).astype(np.int32)
    wprojT_st = np.ascontiguousarray(
        wproj.reshape(32, 128, C).transpose(0, 2, 1)).reshape(32 * C, 128).astype(np.float32)
    scale_st = np.ascontiguousarray(sc.reshape(32, 128).T).astype(np.float32)
    bias_st = np.ascontiguousarray(bi.reshape(32, 128).T).astype(np.float32)
    zp_st = np.ascontiguousarray(zp.reshape(1, C)).astype(np.float32)
    adT_st = np.ascontiguousarray(
        ad.T.astype(np.float32)[permc].reshape(32, 128, 10).transpose(1, 0, 2)
    ).reshape(128, 320)
    gat_st = np.ascontiguousarray(gat.reshape(1, 1)).astype(np.float32)

    jj = np.arange(64)
    theta = 10000.0 ** (-jj / 64.0)
    ang = np.outer(theta, np.arange(T).astype(np.float64))
    cosT = np.concatenate([np.cos(ang), np.cos(ang)], 0).astype(NPBF16)
    sinT = np.concatenate([-np.sin(ang), np.sin(ang)], 0).astype(NPBF16)
    trim = np.triu(np.ones((128, 128))).astype(NPBF16)
    iden = np.eye(128).astype(NPBF16)

    in_maps = []
    for i in range(NCORES):
        q_rows = watt[512 * i:512 * (i + 1)]
        k_rows = watt[C + 512 * i:C + 512 * (i + 1)]
        v_rows = watt[2 * C + 512 * i:2 * C + 512 * (i + 1)]
        qkT = np.concatenate([q_rows, k_rows], 0).T[permc]
        watt_qkT = np.ascontiguousarray(
            qkT.reshape(C, 8, 128).transpose(1, 0, 2)).reshape(8 * C, 128).astype(np.float32)
        wvT_st = np.ascontiguousarray(v_rows.T[permc]).astype(np.float32)
        wakT_st = np.ascontiguousarray(k_rows.T[permc]).astype(np.float32)
        in_maps.append({
            "xT_perm": np.ascontiguousarray(xTp[:, TOWN * i:TOWN * (i + 1)]),
            "xT_nat": np.ascontiguousarray(xT[:, TOWN * i:TOWN * (i + 1)]),
            "watt_qkT": watt_qkT,
            "wvT": wvT_st,
            "wakT": wakT_st,
            "wprojT": wprojT_st,
            "wffn_pk": wffn_st,
            "zp_row": zp_st,
            "scale_pc": scale_st,
            "bias_pc": bias_st,
            "adT": adT_st,
            "gating_in": gat_st,
            "cosT_in": cosT,
            "sinT_in": sinT,
            "trimask_in": trim,
            "ident_in": iden,
        })
    return in_maps


def kernel(**inputs):
    in_maps = _stage(inputs)
    SCHED_TRACE.clear()
    nc = build(dbg=DBG)
    check_deadlock()
    res = run_bass_kernel_spmd(nc, in_maps, core_ids=list(range(NCORES)))
    out = np.empty((1, T, C), dtype=np.float32)
    for i in range(NCORES):
        out[0, TOWN * i:TOWN * (i + 1), :] = res.results[i]["outT"].T
    if DBG:
        kernel._dbg = res.results
    return out


# version 10 (final)
# speedup vs baseline: 1.0421x; 1.0421x over previous
import numpy as np
import ml_dtypes
from contextlib import ExitStack

import concourse.bass as bass
import concourse.mybir as mybir
from concourse.bass_utils import run_bass_kernel_spmd
from concourse.aot_env import allow_ici_all_to_all

F32 = mybir.dt.float32
BF16 = mybir.dt.bfloat16
I32 = mybir.dt.int32
AF = mybir.ActivationFunctionType
ALU = mybir.AluOpType
AX = mybir.AxisListType
NPBF16 = ml_dtypes.bfloat16

C = 4096
T = 8192
HD = 128
NCORES = 8
HLOC = 4
TOWN = 1024
NCC = 32
SCALE = 1.0 / float(np.sqrt(HD))
ASCALE = 1.0 / 64.0
EPS = 1e-5
DBG = False

ENGS = ("sync", "tensor", "vector", "scalar", "gpsimd")

SEM_NAMES = ("pe", "act", "dve", "pool", "cc",
             "dC", "dx0", "dx1", "dx2", "dx3", "dsp", "da0", "da1", "dw0", "dw1", "do0", "do1", "do2", "do3",
             "dWq", "dt0", "dt1", "dr0", "dr1", "dr2", "dr3", "dv0", "dv1",
             "dn0", "dn1", "dh0", "dh1", "dy0", "dy1", "dxn", "dyt",
             "dp0", "dp1", "df0", "df1", "dz0", "dz1", "dD")

SCHED_TRACE = []


class Sched:

    def __init__(self, cnt):
        self.ops = {e: [] for e in ENGS}
        self.cnt = cnt

    def v(self, sem):
        return self.cnt.get(sem, 0)

    def add(self, eng, fn, waits=(), inc=None, n=1):
        self.ops[eng].append((tuple(waits), fn, inc, n))
        if inc is not None:
            self.cnt[inc] = self.v(inc) + n
            return self.cnt[inc]
        return None


def check_deadlock():
    sem = {}
    for name, ops in SCHED_TRACE:
        heads = {e: 0 for e in ops}
        total = sum(len(v) for v in ops.values())
        done = 0
        while done < total:
            progressed = False
            for e, lst in ops.items():
                while heads[e] < len(lst):
                    waits, inc, n = lst[heads[e]]
                    if all(sem.get(s, 0) >= v for (s, v) in waits if v > 0):
                        if inc is not None:
                            sem[inc] = sem.get(inc, 0) + n
                        heads[e] += 1
                        done += 1
                        progressed = True
                    else:
                        break
            if not progressed:
                stuck = {e: (heads[e], lst[heads[e]][0]) for e, lst in ops.items()
                         if heads[e] < len(lst)}
                detail = {e: [(s, v, sem.get(s, 0)) for (s, v) in w if v > 0]
                          for e, (i, w) in stuck.items()}
                raise RuntimeError(f"DEADLOCK in phase {name}: {detail}")
    return True


def emit_block(nc, sems, S, name):
    SCHED_TRACE.append((name, {e: [(w, inc, n) for (w, fn, inc, n) in ops]
                               for e, ops in S.ops.items() if ops}))
    with nc.Block(name, no_gpsimd_drain=True) as blk:
        for eng_name in ENGS:
            ops = S.ops[eng_name]
            if not ops:
                continue

            def make_body(_ops):
                def body(e):
                    for waits, fn, inc, n in _ops:
                        for (s, v) in waits:
                            if v > 0:
                                e.wait_ge(sems[s], v)
                        if fn is not None:
                            instr = fn(e)
                            if inc is not None:
                                instr.then_inc(sems[inc], n)
                return body

            getattr(blk, eng_name)(make_body(ops))


def build(dbg=False):
    nc = bass.Bass("TRN2", target_bir_lowering=False)

    xT_perm = nc.dram_tensor("xT_perm", [C, TOWN], F32, kind="ExternalInput")
    xT_nat = nc.dram_tensor("xT_nat", [C, TOWN], F32, kind="ExternalInput")
    watt_qkT = nc.dram_tensor("watt_qkT", [8 * C, 128], F32, kind="ExternalInput")
    wvT = nc.dram_tensor("wvT", [C, 512], F32, kind="ExternalInput")
    wakT = nc.dram_tensor("wakT", [C, 512], F32, kind="ExternalInput")
    wprojT = nc.dram_tensor("wprojT", [32 * C, 128], F32, kind="ExternalInput")
    wffn_pk = nc.dram_tensor("wffn_pk", [32 * 2048, 128], I32, kind="ExternalInput")
    zp_row = nc.dram_tensor("zp_row", [1, C], F32, kind="ExternalInput")
    scale_pc = nc.dram_tensor("scale_pc", [128, 32], F32, kind="ExternalInput")
    bias_pc = nc.dram_tensor("bias_pc", [128, 32], F32, kind="ExternalInput")
    adT = nc.dram_tensor("adT", [128, NCC * 10], F32, kind="ExternalInput")
    gating_in = nc.dram_tensor("gating_in", [1, 1], F32, kind="ExternalInput")
    cosT_in = nc.dram_tensor("cosT_in", [128, T], BF16, kind="ExternalInput")
    sinT_in = nc.dram_tensor("sinT_in", [128, T], BF16, kind="ExternalInput")
    trimask_in = nc.dram_tensor("trimask_in", [128, 128], BF16, kind="ExternalInput")
    ident_in = nc.dram_tensor("ident_in", [128, 128], BF16, kind="ExternalInput")

    outT = nc.dram_tensor("outT", [C, TOWN], F32, kind="ExternalOutput")

    ag_xn_in = nc.dram_tensor("ag_xn_in", [C, TOWN], BF16)
    ag_xn_out = nc.dram_tensor("ag_xn_out", [NCORES * C, TOWN], BF16)
    a2a_y_in = nc.dram_tensor("a2a_y_in", [NCORES * HLOC * 128, TOWN], BF16)
    a2a_y_out = nc.dram_tensor("a2a_y_out", [C, TOWN], BF16)
    qT_sp = nc.dram_tensor("qT_sp", [128, HLOC * T], BF16)
    kT_sp = nc.dram_tensor("kT_sp", [128, HLOC * T], BF16)
    v_sp = nc.dram_tensor("v_sp", [HLOC * T, 128], BF16)
    ffn_dram = nc.dram_tensor("ffn_dram", [C, TOWN], F32)

    dbg_t = {}
    if dbg:
        dbg_t["dbg_xn"] = nc.dram_tensor("dbg_xn", [C, TOWN], BF16, kind="ExternalOutput")
        dbg_t["dbg_q"] = nc.dram_tensor("dbg_q", [128, HLOC * T], BF16, kind="ExternalOutput")
        dbg_t["dbg_k"] = nc.dram_tensor("dbg_k", [128, HLOC * T], BF16, kind="ExternalOutput")
        dbg_t["dbg_v"] = nc.dram_tensor("dbg_v", [HLOC * T, 128], BF16, kind="ExternalOutput")
        dbg_t["dbg_y"] = nc.dram_tensor("dbg_y", [C, TOWN], BF16, kind="ExternalOutput")
        dbg_t["dbg_ffn"] = nc.dram_tensor("dbg_ffn", [C, TOWN], F32, kind="ExternalOutput")
        dbg_t["dbg_akv"] = nc.dram_tensor("dbg_akv", [10, 1024], BF16, kind="ExternalOutput")
        dbg_t["dbg_b"] = nc.dram_tensor("dbg_b", [128, 8], F32, kind="ExternalOutput")
        dbg_t["dbg_ffn2"] = nc.dram_tensor("dbg_ffn2", [C, TOWN], F32, kind="ExternalOutput")
        dbg_t["dbg_nrm"] = nc.dram_tensor("dbg_nrm", [2, 8], F32, kind="ExternalOutput")
        dbg_t["dbg_akt"] = nc.dram_tensor("dbg_akt", [128, 40], BF16, kind="ExternalOutput")
        dbg_t["dbg_sxnd"] = nc.dram_tensor("dbg_sxnd", [128, NCC * TOWN], BF16,
                                           kind="ExternalOutput")

    cnt = {}

    with ExitStack() as top:
        e = top.enter_context
        sems = {}
        for nm in SEM_NAMES:
            sems[nm] = e(nc.semaphore(nm))

        ones_bf = e(nc.sbuf_tensor("ones_bf", [128, 1], BF16))
        ones_rf = e(nc.sbuf_tensor("ones_rf", [1, 128], F32))
        ones_rbf = e(nc.sbuf_tensor("ones_rbf", [1, 128], BF16))
        ones_cf = e(nc.sbuf_tensor("ones_cf", [128, 1], F32))
        zp_sb = e(nc.sbuf_tensor("zp_sb", [1, C], BF16))
        scale_sb = e(nc.sbuf_tensor("scale_sb", [128, 32], F32))
        bias_sb = e(nc.sbuf_tensor("bias_sb", [128, 32], F32))
        neg_xsum = e(nc.sbuf_tensor("neg_xsum", [1, TOWN], BF16))
        trimask = e(nc.sbuf_tensor("trimask", [128, 128], BF16))
        ident = e(nc.sbuf_tensor("ident", [128, 128], BF16))
        gate_sb = e(nc.sbuf_tensor("gate_sb", [1, 1], F32))
        sakv = e(nc.sbuf_tensor("sakv", [10, 1024], BF16))
        sakT = e(nc.sbuf_tensor("sakT", [128, HLOC * 10], BF16))
        qknorm = e(nc.sbuf_tensor("qknorm", [1, 8], F32))
        aknorm = e(nc.sbuf_tensor("aknorm", [1, 4], F32))
        bb8 = e(nc.sbuf_tensor("bb8", [128, 8], F32))
        epsc = e(nc.sbuf_tensor("epsc", [1, 1], F32))

        with ExitStack() as pha:
            ea = pha.enter_context
            xch2 = ea(nc.sbuf_tensor("xch2", [128, 4 * TOWN], F32))
            mu = ea(nc.sbuf_tensor("mu", [1, TOWN], F32))
            var = ea(nc.sbuf_tensor("var", [1, TOWN], F32))
            lnv = ea(nc.sbuf_tensor("lnv", [1, TOWN], F32))
            rstd = ea(nc.sbuf_tensor("rstd", [1, TOWN], F32))
            nmr = ea(nc.sbuf_tensor("nmr", [1, TOWN], F32))
            rb = ea(nc.sbuf_tensor("rb", [128, TOWN], F32))
            mb = ea(nc.sbuf_tensor("mb", [128, TOWN], F32))
            sxn = ea(nc.sbuf_tensor("sxn", [128, NCC * TOWN], BF16))

            with ExitStack() as a1:
                e1 = a1.enter_context
                sqch2 = e1(nc.sbuf_tensor("sqch2", [128, 2 * TOWN], F32))
                ps_ln = e1(nc.psum_tensor("ps_ln", [1, 4 * 512], F32))
                ps_bc = e1(nc.psum_tensor("ps_bc", [128, 1024], F32))

                S = Sched(cnt)
                for dst, src in ((trimask, trimask_in), (ident, ident_in),
                                 (scale_sb, scale_pc), (bias_sb, bias_pc),
                                 (gate_sb, gating_in)):
                    S.add("sync", lambda ee, d=dst, s=src: ee.dma_start(
                        out=d[:, :], in_=s[:, :]), inc="dC", n=16)
                S.add("gpsimd", lambda ee: ee.dma_start(out=zp_sb[:, :], in_=zp_row[:, :]),
                      inc="dC", n=16)
                S.add("vector", lambda ee: ee.memset(ones_bf[:, :], 1.0), inc="dve")
                S.add("vector", lambda ee: ee.memset(ones_rf[:, :], 1.0), inc="dve")
                S.add("vector", lambda ee: ee.memset(ones_rbf[:, :], 1.0), inc="dve")
                S.add("vector", lambda ee: ee.memset(ones_cf[:, :], 1.0), inc="dve")
                S.add("vector", lambda ee: ee.memset(qknorm[:, :], 0.0), inc="dve")
                S.add("vector", lambda ee: ee.memset(aknorm[:, :], 0.0), inc="dve")
                S.add("vector", lambda ee: ee.memset(epsc[:, :], float(EPS)), inc="dve")
                memset_done = S.v("dve")

                pe_after = {}
                act_after = {}
                for cc in range(NCC):
                    b = cc % 2
                    dxb = "dx%d" % b
                    waits = []
                    if cc >= 2:
                        waits = [("pe", pe_after[cc - 2]), ("act", act_after[cc - 2])]
                    ld = S.add("sync",
                               lambda ee, cc=cc, b=b: ee.dma_start(
                                   out=xch2[:, b * TOWN:(b + 1) * TOWN],
                                   in_=xT_perm[128 * cc:128 * (cc + 1), :]),
                               waits=waits, inc=dxb, n=16)
                    aw = [(dxb, ld)]
                    if cc >= 2:
                        aw.append(("pe", pe_after[cc - 2]))
                    act_after[cc] = S.add(
                        "scalar",
                        lambda ee, b=b: ee.activation(
                            out=sqch2[:, b * TOWN:(b + 1) * TOWN],
                            in_=xch2[:, b * TOWN:(b + 1) * TOWN], func=AF.Square),
                        waits=aw, inc="act")
                    st = (cc == 0)
                    sp = (cc == NCC - 1)
                    for tt in range(2):
                        S.add("tensor",
                              lambda ee, b=b, tt=tt, st=st, sp=sp: ee.matmul(
                                  out=ps_ln[0:1, 512 * tt:512 * (tt + 1)],
                                  lhsT=ones_cf[:, 0:1],
                                  rhs=xch2[:, b * TOWN + 512 * tt: b * TOWN + 512 * (tt + 1)],
                                  start=st, stop=sp),
                              waits=[(dxb, ld)] if tt == 0 else ())
                    for tt in range(2):
                        w = [("act", act_after[cc])] if tt == 0 else ()
                        r = S.add(
                            "tensor",
                            lambda ee, b=b, tt=tt, st=st, sp=sp: ee.matmul(
                                out=ps_ln[0:1, 1024 + 512 * tt:1024 + 512 * (tt + 1)],
                                lhsT=ones_cf[:, 0:1],
                                rhs=sqch2[:, b * TOWN + 512 * tt: b * TOWN + 512 * (tt + 1)],
                                start=st, stop=sp),
                            waits=w, inc=("pe" if tt == 1 else None))
                        if r is not None:
                            pe_after[cc] = r
                ln_mms_done = S.v("pe")

                for tt in range(2):
                    S.add("vector",
                          lambda ee, tt=tt: ee.tensor_scalar(
                              out=mu[0:1, 512 * tt:512 * (tt + 1)],
                              in0=ps_ln[0:1, 512 * tt:512 * (tt + 1)],
                              scalar1=1.0 / C, scalar2=None, op0=ALU.mult),
                          waits=[("pe", ln_mms_done)] if tt == 0 else (), inc="dve")
                    S.add("vector",
                          lambda ee, tt=tt: ee.tensor_scalar(
                              out=var[0:1, 512 * tt:512 * (tt + 1)],
                              in0=ps_ln[0:1, 1024 + 512 * tt:1024 + 512 * (tt + 1)],
                              scalar1=1.0 / C, scalar2=None, op0=ALU.mult),
                          inc="dve")
                S.add("vector", lambda ee: ee.tensor_tensor(
                    out=lnv[0:1, :], in0=mu[0:1, :], in1=mu[0:1, :], op=ALU.mult), inc="dve")
                S.add("vector", lambda ee: ee.tensor_tensor(
                    out=var[0:1, :], in0=var[0:1, :], in1=lnv[0:1, :], op=ALU.subtract),
                    inc="dve")
                stats_dve = S.v("dve")
                S.add("scalar", lambda ee: ee.activation(
                    out=lnv[0:1, :], in_=var[0:1, :], func=AF.Ln, bias=epsc[0:1, 0:1]),
                    waits=[("dve", stats_dve)], inc="act")
                S.add("scalar", lambda ee: ee.activation(
                    out=rstd[0:1, :], in_=lnv[0:1, :], func=AF.Exp, scale=-0.5), inc="act")
                rstd_act = S.v("act")
                S.add("vector", lambda ee: ee.scalar_tensor_tensor(
                    out=nmr[0:1, :], in0=mu[0:1, :], scalar=-1.0, in1=rstd[0:1, :],
                    op0=ALU.mult, op1=ALU.mult),
                    waits=[("act", rstd_act)], inc="dve")
                nmr_dve = S.v("dve")

                for tt in range(2):
                    S.add("tensor", lambda ee, tt=tt: ee.matmul(
                        out=ps_bc[:, 512 * tt:512 * (tt + 1)], lhsT=ones_rf[0:1, :],
                        rhs=rstd[0:1, 512 * tt:512 * (tt + 1)], start=True, stop=True),
                        waits=[("act", rstd_act), ("dve", memset_done)] if tt == 0 else (),
                        inc="pe")
                rb_mm = S.v("pe")
                S.add("vector", lambda ee: ee.tensor_copy(out=rb[:, :], in_=ps_bc[:, :]),
                      waits=[("pe", rb_mm)], inc="dve")
                rb_done = S.v("dve")
                for tt in range(2):
                    S.add("tensor", lambda ee, tt=tt: ee.matmul(
                        out=ps_bc[:, 512 * tt:512 * (tt + 1)], lhsT=ones_rf[0:1, :],
                        rhs=nmr[0:1, 512 * tt:512 * (tt + 1)], start=True, stop=True),
                        waits=[("dve", max(nmr_dve, rb_done))] if tt == 0 else (), inc="pe")
                mb_mm = S.v("pe")
                S.add("vector", lambda ee: ee.tensor_copy(out=mb[:, :], in_=ps_bc[:, :]),
                      waits=[("pe", mb_mm)], inc="dve")

                emit_block(nc, sems, S, "A1")

            with ExitStack() as a2:
                e2 = a2.enter_context
                xnt = e2(nc.sbuf_tensor("xnt", [128, TOWN], F32))
                swd = e2(nc.sbuf_tensor("swd", [128, 2 * 2048], I32))
                silo = e2(nc.sbuf_tensor("silo", [128, 2048], I32))
                sihi = e2(nc.sbuf_tensor("sihi", [128, 2048], I32))
                wnib = e2(nc.sbuf_tensor("wnib", [128, 2 * C], BF16))
                fbuf = e2(nc.sbuf_tensor("fbuf", [128, 4 * 512], F32))
                sad = e2(nc.sbuf_tensor("sad", [128, NCC * 10], BF16))
                swbig = e2(nc.sbuf_tensor("swbig", [128, 16 * 512], BF16))
                sqa = e2(nc.sbuf_tensor("sqa", [128, 40], BF16))
                ps_xs = e2(nc.psum_tensor("ps_xs", [1, 1024], F32))
                ps_ak = e2(nc.psum_tensor("ps_ak", [10, 1024], F32))
                ps_tr = e2(nc.psum_tensor("ps_tr", [128, 16], BF16))
                ps_nm = e2(nc.psum_tensor("ps_nm", [1, 64], F32))
                ps_f = e2(nc.psum_tensor("ps_f", [128, 2 * 512], F32))

                S = Sched(cnt)
                adl_t = S.add("gpsimd", lambda ee: ee.dma_start(
                    out=sad[:, :], in_=adT[:, :]), inc="dC", n=16)
                dC_A2 = S.v("dC")

                dve_p2 = {}
                for cc in range(NCC):
                    b = cc % 4
                    dxb = "dx%d" % b
                    waits = []
                    if cc >= 4:
                        waits = [("dve", dve_p2[cc - 4])]
                    ld = S.add("sync",
                               lambda ee, cc=cc, b=b: ee.dma_start(
                                   out=xch2[:, b * TOWN:(b + 1) * TOWN],
                                   in_=xT_perm[128 * cc:128 * (cc + 1), :]),
                               waits=waits, inc=dxb, n=16)
                    S.add("vector", lambda ee, b=b: ee.tensor_tensor(
                        out=xnt[:, :], in0=xch2[:, b * TOWN:(b + 1) * TOWN],
                        in1=rb[:, :], op=ALU.mult),
                        waits=[(dxb, ld)])
                    dve_p2[cc] = S.add("vector", lambda ee, cc=cc: ee.tensor_tensor(
                        out=sxn[:, TOWN * cc:TOWN * (cc + 1)], in0=xnt[:, :],
                        in1=mb[:, :], op=ALU.add), inc="dve")
                    S.add("scalar", lambda ee, cc=cc: ee.dma_start(
                        out=ag_xn_in[128 * cc:128 * (cc + 1), :],
                        in_=sxn[:, TOWN * cc:TOWN * (cc + 1)]),
                        waits=[("dve", dve_p2[cc])], inc="dsp", n=16)
                spill_done = S.v("dsp")

                S.add("gpsimd", lambda ee: ee.collective_compute(
                    "AllGather", ALU.bypass,
                    replica_groups=[list(range(NCORES))],
                    ins=[ag_xn_in.ap().opt()],
                    outs=[ag_xn_out.ap().opt()]),
                    waits=[("dsp", spill_done)], inc="cc")

                for cc in range(NCC):
                    for tt in range(2):
                        S.add("tensor", lambda ee, cc=cc, tt=tt: ee.matmul(
                            out=ps_xs[0:1, 512 * tt:512 * (tt + 1)], lhsT=ones_bf[:, 0:1],
                            rhs=sxn[:, TOWN * cc + 512 * tt:TOWN * cc + 512 * (tt + 1)],
                            start=(cc == 0), stop=(cc == NCC - 1)),
                            waits=[("dve", dve_p2[cc]), ("dve", memset_done)]
                            if tt == 0 else (),
                            inc=("pe" if (cc == NCC - 1 and tt == 1) else None))
                xs_mm = S.v("pe")
                S.add("vector", lambda ee: ee.tensor_scalar(
                    out=neg_xsum[0:1, :], in0=ps_xs[0:1, :], scalar1=-1.0, scalar2=None,
                    op0=ALU.mult), waits=[("pe", xs_mm)], inc="dve")
                negxs_done = S.v("dve")

                def ffn_chunk(S, oc, slot, sxn_src, swd, silo, sihi, wnib, fbuf, ps_f,
                              key, waits_extra=()):
                    b = slot % 2
                    dwb = "dw%d" % b
                    w0 = list(waits_extra)
                    prev_dve = S.cnt.get("%s_dve_%d" % (key, slot - 2), 0)
                    if prev_dve:
                        w0.append(("dve", prev_dve))
                    wld = S.add("gpsimd", lambda ee, oc=oc, b=b: ee.dma_start(
                        out=swd.ap()[:, 2048 * b:2048 * (b + 1)].rearrange(
                            "p (a o) -> p a o", a=16),
                        in_=wffn_pk.ap()[2048 * oc:2048 * (oc + 1), :].rearrange(
                            "(a p) o -> p a o", p=128)),
                        waits=w0, inc=dwb, n=16)
                    lo = S.add("vector", lambda ee, b=b: ee.tensor_scalar(
                        out=silo[:, :], in0=swd[:, 2048 * b:2048 * (b + 1)],
                        scalar1=15, scalar2=None, op0=ALU.bitwise_and),
                        waits=[(dwb, wld)], inc="dve")
                    hi = S.add("vector", lambda ee, b=b: ee.tensor_scalar(
                        out=sihi[:, :], in0=swd[:, 2048 * b:2048 * (b + 1)],
                        scalar1=4, scalar2=15, op0=ALU.logical_shift_right,
                        op1=ALU.bitwise_and), inc="dve")
                    pw = [("pe", S.cnt.get("%s_pe_%d" % (key, slot - 2), 0))]
                    S.add("vector", lambda ee, b=b: ee.tensor_copy(
                        out=wnib[:, C * b:C * b + 2048], in_=silo[:, :]),
                        waits=pw, inc="dve")
                    ch = S.add("vector", lambda ee, b=b: ee.tensor_copy(
                        out=wnib[:, C * b + 2048:C * b + C], in_=sihi[:, :]),
                        inc="dve")
                    last_pe = 0
                    for tt in range(2):
                        prev_act = S.cnt.get("%s_actt_%d_%d" % (key, slot - 1, tt), 0)
                        for cc2 in range(NCC):
                            w = []
                            if cc2 == 0:
                                w = [("dve", ch)]
                                w.append(("act", prev_act))
                            S.add("tensor", lambda ee, b=b, tt=tt, cc2=cc2: ee.matmul(
                                out=ps_f[:, 512 * tt:512 * (tt + 1)],
                                lhsT=wnib[:, C * b + 128 * cc2:C * b + 128 * (cc2 + 1)],
                                rhs=sxn_src[:, TOWN * cc2 + 512 * tt:
                                            TOWN * cc2 + 512 * (tt + 1)],
                                start=(cc2 == 0), stop=False), waits=w)
                        last_pe = S.add("tensor", lambda ee, oc=oc, tt=tt: ee.matmul(
                            out=ps_f[:, 512 * tt:512 * (tt + 1)],
                            lhsT=zp_sb[0:1, 128 * oc:128 * (oc + 1)],
                            rhs=neg_xsum[0:1, 512 * tt:512 * (tt + 1)],
                            start=False, stop=True), inc="pe")
                        fr = 2 * b + tt
                        dof = "do%d" % fr
                        av = S.add("scalar", lambda ee, oc=oc, fr=fr, tt=tt: ee.activation(
                            out=fbuf[:, 512 * fr:512 * (fr + 1)],
                            in_=ps_f[:, 512 * tt:512 * (tt + 1)], func=AF.Identity,
                            scale=scale_sb[:, oc:oc + 1], bias=bias_sb[:, oc:oc + 1]),
                            waits=[("pe", last_pe),
                                   (dof, S.cnt.get("%s_dma_%d_%d" % (key, slot - 2, tt), 0))],
                            inc="act")
                        S.cnt["%s_actt_%d_%d" % (key, slot, tt)] = av
                        S.cnt["%s_dma_%d_%d" % (key, slot, tt)] = S.add(
                            "sync", lambda ee, oc=oc, fr=fr, tt=tt: ee.dma_start(
                                out=ffn_dram[128 * oc:128 * (oc + 1),
                                             512 * tt:512 * (tt + 1)],
                                in_=fbuf[:, 512 * fr:512 * (fr + 1)]),
                            waits=[("act", av)], inc=dof, n=16)
                    S.cnt["%s_pe_%d" % (key, slot)] = last_pe
                    S.cnt["%s_dve_%d" % (key, slot)] = hi

                for oc in range(16):
                    extra = []
                    if oc == 0:
                        extra = [("dve", max(negxs_done, dve_p2[NCC - 1])),
                                 ("dC", dC_A2)]
                    ffn_chunk(S, oc, oc, sxn, swd, silo, sihi, wnib, fbuf, ps_f,
                              "_f1", waits_extra=extra)

                ak_mm = {}
                for ot in range(2):
                    srct = wakT if ot == 0 else wvT
                    for h in range(2):
                        waits = []
                        if (ot, h) != (0, 0):
                            waits = [("pe", ak_mm[(ot, 16 * h - 1) if h == 1 else
                                                  (0, NCC - 1)])]
                        wl = S.add("gpsimd", lambda ee, srct=srct, h=h: ee.dma_start(
                            out=swbig.ap().rearrange("p (a o) -> p a o", a=16),
                            in_=srct.ap()[2048 * h:2048 * (h + 1), :].rearrange(
                                "(a p) o -> p a o", p=128)),
                            waits=waits, inc="da0", n=16)
                        for cc in range(16 * h, 16 * (h + 1)):
                            mw = []
                            if cc == 16 * h:
                                mw = [("da0", wl), ("dC", dC_A2)]
                            ak_mm[(ot, cc)] = S.add(
                                "tensor", lambda ee, ot=ot, cc=cc, h=h: ee.matmul(
                                    out=ps_ak[0:10, 512 * ot:512 * (ot + 1)],
                                    lhsT=sad[:, 10 * cc:10 * (cc + 1)],
                                    rhs=swbig[:, 512 * (cc - 16 * h):
                                              512 * (cc - 16 * h + 1)],
                                    start=(cc == 0), stop=(cc == NCC - 1)),
                                waits=mw, inc="pe")
                S.add("vector", lambda ee: ee.tensor_copy(out=sakv[:, :], in_=ps_ak[0:10, :]),
                      waits=[("pe", ak_mm[(1, NCC - 1)])], inc="dve")
                sakv_done = S.v("dve")

                for hl in range(HLOC):
                    tr = S.add("tensor", lambda ee, hl=hl: ee.transpose(
                        out=ps_tr[:, 0:10], in_=sakv[0:10, 128 * hl:128 * (hl + 1)],
                        identity=ident[0:10, 0:10]),
                        waits=[("dve", S.v("dve")), ("dC", dC_A2)], inc="pe")
                    cpd = S.add("vector", lambda ee, hl=hl: ee.tensor_copy(
                        out=sakT[:, 10 * hl:10 * (hl + 1)], in_=ps_tr[:, 0:10]),
                        waits=[("pe", tr)], inc="dve")
                    S.add("scalar", lambda ee, hl=hl: ee.activation(
                        out=sqa[:, 10 * hl:10 * (hl + 1)],
                        in_=sakT[:, 10 * hl:10 * (hl + 1)],
                        func=AF.Square),
                        waits=[("dve", cpd)], inc="act")
                nm_last = S.add("tensor", lambda ee: ee.matmul(
                    out=ps_nm[0:1, 0:40], lhsT=ones_bf[:, 0:1],
                    rhs=sqa[:, 0:40],
                    start=True, stop=True), waits=[("act", S.v("act"))], inc="pe")
                for hl in range(HLOC):
                    S.add("vector", lambda ee, hl=hl: ee.reduce_max(
                        out=aknorm[0:1, hl:hl + 1], in_=ps_nm[0:1, 10 * hl:10 * (hl + 1)],
                        axis=AX.X),
                        waits=[("pe", nm_last)] if hl == 0 else (), inc="dve")


                emit_block(nc, sems, S, "A2")

        with ExitStack() as phb:
            eb = phb.enter_context
            cosT = eb(nc.sbuf_tensor("cosT", [128, T], BF16))
            sinT = eb(nc.sbuf_tensor("sinT", [128, T], BF16))
            sxt = eb(nc.sbuf_tensor("sxt", [128, 2 * NCC * 512], BF16))
            scr1 = eb(nc.sbuf_tensor("scr1", [128, 512], F32))
            scr2 = eb(nc.sbuf_tensor("scr2", [128, 512], F32))
            rbuf = eb(nc.sbuf_tensor("rbuf", [128, 4 * 512], BF16))

            with ExitStack() as b1:
                e1 = b1.enter_context
                swqk = e1(nc.sbuf_tensor("swqk", [128, 8 * C], BF16))
                ps_qk = e1(nc.psum_tensor("ps_qk", [128, 8 * 512], F32))

                S = Sched(cnt)
                S.add("sync", lambda ee: ee.dma_start(out=cosT[:, :], in_=cosT_in[:, :]),
                      inc="dC", n=16)
                S.add("sync", lambda ee: ee.dma_start(
                    out=sinT[:, :], in_=sinT_in[:, :]), inc="dC", n=16)
                dC_B1 = S.v("dC")
                for b in range(8):
                    S.add("gpsimd", lambda ee, b=b: ee.dma_start(
                        out=swqk.ap()[:, C * b:C * (b + 1)].rearrange(
                            "p (a o) -> p a o", a=NCC),
                        in_=watt_qkT.ap()[C * b:C * (b + 1), :].rearrange(
                            "(a p) o -> p a o", p=128)), inc="dWq", n=16)
                dWq_all = S.v("dWq")
                rope_done = {}
                for j in range(16):
                    b = j % 2
                    dtb = "dt%d" % b
                    waits = [("cc", 1)] if j == 0 else []
                    if j >= 2:
                        waits.append(("pe", S.cnt["_b1_pe_%d" % (j - 2)]))
                    ld = S.add("sync", lambda ee, j=j, b=b: ee.dma_start(
                        out=sxt.ap()[:, NCC * 512 * b:NCC * 512 * (b + 1)].rearrange(
                            "p (a t) -> p a t", a=NCC),
                        in_=ag_xn_out.ap()[C * (j // 2):C * (j // 2 + 1),
                                           512 * (j % 2):512 * (j % 2) + 512].rearrange(
                            "(a p) t -> p a t", p=128)),
                        waits=waits, inc=dtb, n=16)
                    for oc in range(8):
                        w = [(dtb, ld)]
                        if oc == 0 and j == 0:
                            w.append(("dWq", dWq_all))
                        if j >= 1:
                            w.append(("dve", rope_done[(j - 1, oc)]))
                        for cc in range(NCC):
                            S.add("tensor", lambda ee, oc=oc, b=b, cc=cc: ee.matmul(
                                out=ps_qk[:, 512 * oc:512 * (oc + 1)],
                                lhsT=swqk[:, C * oc + 128 * cc:C * oc + 128 * (cc + 1)],
                                rhs=sxt[:, NCC * 512 * b + 512 * cc:
                                        NCC * 512 * b + 512 * (cc + 1)],
                                start=(cc == 0), stop=(cc == NCC - 1)),
                                waits=w if cc == 0 else (),
                                inc=("pe" if cc == NCC - 1 else None))
                        mmn = S.v("pe")
                        rb_ = (8 * j + oc) % 4
                        drb = "dr%d" % rb_
                        rw = [("pe", mmn), ("dC", dC_B1)]
                        prev_use = 8 * j + oc - 4
                        if prev_use >= 0:
                            rw.append((drb, S.cnt["_b1_spd_%d" % prev_use]))
                        S.add("vector", lambda ee, oc=oc, j=j: ee.tensor_tensor(
                            out=scr1[:, :], in0=ps_qk[:, 512 * oc:512 * (oc + 1)],
                            in1=cosT[:, 512 * j:512 * (j + 1)], op=ALU.mult),
                            waits=rw)
                        S.add("vector", lambda ee, oc=oc, j=j: ee.tensor_tensor(
                            out=scr2[0:64, :],
                            in0=ps_qk[64:128, 512 * oc:512 * (oc + 1)],
                            in1=sinT[0:64, 512 * j:512 * (j + 1)], op=ALU.mult))
                        S.add("vector", lambda ee, oc=oc, j=j: ee.tensor_tensor(
                            out=scr2[64:128, :],
                            in0=ps_qk[0:64, 512 * oc:512 * (oc + 1)],
                            in1=sinT[64:128, 512 * j:512 * (j + 1)], op=ALU.mult))
                        rope_done[(j, oc)] = S.add(
                            "vector", lambda ee, rb_=rb_: ee.tensor_tensor(
                                out=rbuf[:, 512 * rb_:512 * (rb_ + 1)], in0=scr1[:, :],
                                in1=scr2[:, :], op=ALU.add), inc="dve")
                        dst = qT_sp if oc < 4 else kT_sp
                        hoff = (oc % 4) * T + 512 * j
                        S.cnt["_b1_spd_%d" % (8 * j + oc)] = S.add(
                            "gpsimd", lambda ee, dst=dst, hoff=hoff, rb_=rb_: ee.dma_start(
                                out=dst[:, hoff:hoff + 512],
                                in_=rbuf[:, 512 * rb_:512 * (rb_ + 1)]),
                            waits=[("dve", rope_done[(j, oc)])], inc=drb, n=16)
                    S.cnt["_b1_pe_%d" % j] = S.v("pe")
                emit_block(nc, sems, S, "B1")

            with ExitStack() as b2:
                e2 = b2.enter_context
                swv = e2(nc.sbuf_tensor("swv", [128, NCC * 512], BF16))
                vbuf = e2(nc.sbuf_tensor("vbuf", [128, 2 * 512], BF16))
                ps_v = e2(nc.psum_tensor("ps_v", [128, 2 * 512], F32))

                S = Sched(cnt)
                S.add("gpsimd", lambda ee: ee.dma_start(
                    out=swv.ap().rearrange("p (a o) -> p a o", a=NCC),
                    in_=wvT.ap().rearrange("(a p) o -> p a o", p=128)), inc="dWq", n=16)
                dWq_b2 = S.v("dWq")
                for j in range(16):
                    b = j % 2
                    dtb = "dt%d" % b
                    waits = []
                    if j >= 2:
                        waits.append(("pe", S.cnt["_b2_pe_%d" % (j - 2)]))
                    ld = S.add("sync", lambda ee, j=j, b=b: ee.dma_start(
                        out=sxt.ap()[:, NCC * 512 * b:NCC * 512 * (b + 1)].rearrange(
                            "p (a t) -> p a t", a=NCC),
                        in_=ag_xn_out.ap()[C * (j // 2):C * (j // 2 + 1),
                                           512 * (j % 2):512 * (j % 2) + 512].rearrange(
                            "(a p) t -> p a t", p=128)),
                        waits=waits, inc=dtb, n=16)
                    for s in range(4):
                        it = 4 * j + s
                        sb_ = it % 2
                        dvb = "dv%d" % sb_
                        w = [(dtb, ld)]
                        if it == 0:
                            w.append(("dWq", dWq_b2))
                        if it >= 2:
                            w.append(("act", S.cnt["_b2_ev_%d" % (it - 2)]))
                        for cc in range(NCC):
                            S.add("tensor", lambda ee, b=b, s=s, cc=cc, sb_=sb_: ee.matmul(
                                out=ps_v[:, 512 * sb_:512 * (sb_ + 1)],
                                lhsT=sxt[:, NCC * 512 * b + 512 * cc + 128 * s:
                                         NCC * 512 * b + 512 * cc + 128 * (s + 1)],
                                rhs=swv[:, 512 * cc:512 * (cc + 1)],
                                start=(cc == 0), stop=(cc == NCC - 1)),
                                waits=w if cc == 0 else (),
                                inc=("pe" if cc == NCC - 1 else None))
                        mmn = S.v("pe")
                        evw = [("pe", mmn)]
                        if it >= 2:
                            evw.append((dvb, S.cnt["_b2_sp_%d" % (it - 2)]))
                        ev = S.add("scalar", lambda ee, sb_=sb_: ee.copy(
                            out=vbuf[:, 512 * sb_:512 * (sb_ + 1)],
                            in_=ps_v[:, 512 * sb_:512 * (sb_ + 1)]),
                            waits=evw, inc="act")
                        S.cnt["_b2_ev_%d" % it] = ev
                        for hl in range(HLOC):
                            S.add("gpsimd", lambda ee, hl=hl, it=it, sb_=sb_: ee.dma_start(
                                out=v_sp[T * hl + 128 * it:T * hl + 128 * (it + 1), :],
                                in_=vbuf[:, 512 * sb_ + 128 * hl:512 * sb_ + 128 * (hl + 1)]),
                                waits=[("act", ev)] if hl == 0 else (), inc=dvb, n=16)
                        S.cnt["_b2_sp_%d" % it] = S.v(dvb)
                    S.cnt["_b2_pe_%d" % j] = S.v("pe")
                emit_block(nc, sems, S, "B2")

        with ExitStack() as phc:
            ec = phc.enter_context
            kh = ec(nc.sbuf_tensor("kh", [128, 2 * T], BF16))
            vh = ec(nc.sbuf_tensor("vh", [128, 2 * T], BF16))
            qh = ec(nc.sbuf_tensor("qh", [128, 2 * T], BF16))
            pg = ec(nc.sbuf_tensor("pg", [128, 2 * 1024], BF16))
            pa = ec(nc.sbuf_tensor("pa", [10, 512], BF16))
            sdm = ec(nc.sbuf_tensor("sdm", [1, 512], BF16))
            sdag = ec(nc.sbuf_tensor("sdag", [1, 512], BF16))
            lnd = ec(nc.sbuf_tensor("lnd", [1, 512], F32))
            sbd = ec(nc.sbuf_tensor("sbd", [128, 512], F32))
            sbda = ec(nc.sbuf_tensor("sbda", [128, 512], F32))
            yt1 = ec(nc.sbuf_tensor("yt1", [128, 512], F32))
            yt2 = ec(nc.sbuf_tensor("yt2", [128, 512], F32))
            ybuf = ec(nc.sbuf_tensor("ybuf", [128, 2 * 512], BF16))
            nbuf = ec(nc.sbuf_tensor("nbuf", [128, 2 * 2048], BF16))
            nsq = ec(nc.sbuf_tensor("nsq", [128, 2048], BF16))
            bt8 = ec(nc.sbuf_tensor("bt8", [1, 8], F32))
            nseg = ec(nc.sbuf_tensor("nseg", [1, 32], F32))
            tm11 = ec(nc.sbuf_tensor("tm11", [1, 1], F32))
            ps_s = ec(nc.psum_tensor("ps_s", [128, 2 * 1024], F32))
            ps_y = ec(nc.psum_tensor("ps_y", [128, 512], F32))
            ps_ya = ec(nc.psum_tensor("ps_ya", [128, 512], F32))
            ps_d = ec(nc.psum_tensor("ps_d", [1, 512], F32))
            ps_da = ec(nc.psum_tensor("ps_da", [1, 512], F32))

            S = Sched(cnt)
            spill_fin = [("dr0", S.v("dr0")), ("dr1", S.v("dr1")),
                         ("dr2", S.v("dr2")), ("dr3", S.v("dr3"))]
            vsp_fin = [("dv0", S.v("dv0")), ("dv1", S.v("dv1"))]
            for idx, (src, base) in enumerate([(qT_sp, 0), (kT_sp, 4)]):
                for hl in range(HLOC):
                    for seg in range(4):
                        it = idx * 16 + hl * 4 + seg
                        b = it % 2
                        dnb = "dn%d" % b
                        w = list(spill_fin) if it == 0 else []
                        if it >= 2:
                            w.append(("act", S.cnt["_c_nsq_%d" % (it - 2)]))
                        ld = S.add("sync", lambda ee, src=src, hl=hl, seg=seg, b=b:
                                   ee.dma_start(
                                       out=nbuf[:, 2048 * b:2048 * (b + 1)],
                                       in_=src[:, T * hl + 2048 * seg:
                                               T * hl + 2048 * (seg + 1)]),
                                   waits=w, inc=dnb, n=16)
                        sqw = [(dnb, ld)]
                        if it >= 1:
                            sqw.append(("pe", S.cnt["_c_nmm_%d" % (it - 1)]))
                        S.cnt["_c_nsq_%d" % it] = S.add(
                            "scalar", lambda ee, b=b: ee.activation(
                                out=nsq[:, :], in_=nbuf[:, 2048 * b:2048 * (b + 1)],
                                func=AF.Square),
                            waits=sqw, inc="act")
                        mmw = [("act", S.cnt["_c_nsq_%d" % it])]
                        if it >= 1:
                            mmw.append(("dve", S.cnt["_c_nmx_%d" % (it - 1)]))
                        for k4 in range(4):
                            S.add("tensor", lambda ee, k4=k4: ee.matmul(
                                out=ps_s[0:1, 512 * k4:512 * (k4 + 1)],
                                lhsT=ones_bf[:, 0:1], rhs=nsq[:, 512 * k4:512 * (k4 + 1)],
                                start=True, stop=True),
                                waits=mmw if k4 == 0 else (),
                                inc=("pe" if k4 == 3 else None))
                        S.cnt["_c_nmm_%d" % it] = S.v("pe")
                        S.cnt["_c_nmx_%d" % it] = S.add(
                            "vector", lambda ee, it=it: ee.reduce_max(
                                out=nseg[0:1, it:it + 1], in_=ps_s[0:1, 0:2048],
                                axis=AX.X),
                            waits=[("pe", S.cnt["_c_nmm_%d" % it])], inc="dve")
            for oc8 in range(8):
                S.add("vector", lambda ee, oc8=oc8: ee.reduce_max(
                    out=qknorm[0:1, oc8:oc8 + 1], in_=nseg[0:1, 4 * oc8:4 * (oc8 + 1)],
                    axis=AX.X), inc="dve")
            S.add("vector", lambda ee: ee.engine_nop(), inc="dve")
            S.add("vector", lambda ee: ee.engine_nop(), inc="dve")
            norm_dve = S.v("dve")

            S.add("vector", lambda ee: ee.tensor_tensor(
                out=bt8[0:1, 0:4], in0=qknorm[0:1, 0:4], in1=qknorm[0:1, 4:8],
                op=ALU.mult), waits=[("dve", norm_dve)], inc="dve")
            S.add("vector", lambda ee: ee.tensor_tensor(
                out=bt8[0:1, 4:8], in0=qknorm[0:1, 0:4], in1=aknorm[0:1, 0:4],
                op=ALU.mult), inc="dve")
            b_dve = S.v("dve")
            S.add("scalar", lambda ee: ee.activation(
                out=bt8[0:1, :], in_=bt8[0:1, :], func=AF.Ln, bias=epsc[0:1, 0:1]),
                waits=[("dve", b_dve)], inc="act")
            S.add("scalar", lambda ee: ee.activation(
                out=bt8[0:1, :], in_=bt8[0:1, :], func=AF.Exp, scale=0.5), inc="act")
            b_act = S.v("act")
            S.add("vector", lambda ee: ee.tensor_scalar(
                out=bt8[0:1, 0:4], in0=bt8[0:1, 0:4], scalar1=-1.02 * SCALE,
                scalar2=-0.1, op0=ALU.mult, op1=ALU.add),
                waits=[("act", b_act)], inc="dve")
            S.add("vector", lambda ee: ee.tensor_scalar(
                out=bt8[0:1, 4:8], in0=bt8[0:1, 4:8], scalar1=-1.02 * ASCALE,
                scalar2=-0.1, op0=ALU.mult, op1=ALU.add), inc="dve")
            nb_dve = S.v("dve")
            bmm = S.add("tensor", lambda ee: ee.matmul(
                out=ps_y[:, 0:8], lhsT=ones_rf[0:1, :], rhs=bt8[0:1, :],
                start=True, stop=True), waits=[("dve", nb_dve)], inc="pe")
            S.add("vector", lambda ee: ee.tensor_copy(out=bb8[:, :], in_=ps_y[:, 0:8]),
                  waits=[("pe", bmm)], inc="dve")
            bb_done = S.v("dve")

            for hl in range(HLOC):
                b = hl % 2
                dhb = "dh%d" % b
                w = [("dve", bb_done)]
                if hl == 0:
                    w += vsp_fin
                if hl >= 2:
                    w.append(("pe", S.cnt["_c_hpe_%d" % (hl - 2)]))
                S.add("sync", lambda ee, hl=hl, b=b: ee.dma_start(
                    out=qh[:, T * b:T * (b + 1)], in_=qT_sp[:, T * hl:T * (hl + 1)]),
                    waits=w, inc=dhb, n=16)
                S.add("sync", lambda ee, hl=hl, b=b: ee.dma_start(
                    out=kh[:, T * b:T * (b + 1)], in_=kT_sp[:, T * hl:T * (hl + 1)]),
                    inc=dhb, n=16)
                l3 = S.add("sync", lambda ee, hl=hl, b=b: ee.dma_start(
                    out=vh.ap()[:, T * b:T * (b + 1)].rearrange("p (a d) -> p a d", a=64),
                    in_=v_sp.ap()[T * hl:T * (hl + 1), :].rearrange(
                        "(a p) d -> p a d", p=128)),
                    inc=dhb, n=16)
                for qb in range(16):
                    qoff = T * b + 512 * qb
                    G = 2 * (qb + 1)
                    saw = [(dhb, l3), ("dve", S.cnt.get("_c_sbd", 0))]
                    sa = S.add("tensor", lambda ee, hl=hl, qoff=qoff: ee.matmul(
                        out=ps_s[0:10, 0:512], lhsT=sakT[:, 10 * hl:10 * (hl + 1)],
                        rhs=qh[:, qoff:qoff + 512], start=True, stop=True),
                        waits=saw, inc="pe")
                    expa = S.add("scalar", lambda ee, hl=hl: ee.activation(
                        out=pa[0:10, :], in_=ps_s[0:10, 0:512], func=AF.Exp,
                        bias=bb8[0:10, 4 + hl:5 + hl], scale=ASCALE),
                        waits=[("pe", sa)], inc="act")
                    dw_ = [("act", expa), ("dve", S.cnt.get("_c_yt2", 0))]
                    S.add("tensor", lambda ee: ee.matmul(
                        out=ps_da[0:1, :], lhsT=ones_bf[0:10, 0:1], rhs=pa[0:10, :],
                        start=True, stop=True), waits=dw_, inc="pe")
                    S.add("tensor", lambda ee, hl=hl: ee.matmul(
                        out=ps_ya[:, :],
                        lhsT=sakv[0:10, 512 + 128 * hl:512 + 128 * (hl + 1)],
                        rhs=pa[0:10, :], start=True, stop=True), inc="pe")
                    U = 4 * (qb + 1)
                    exp_cnt = {}
                    mask_cnt = {}

                    def emit_S(u):
                        sl = u % 4
                        sw = [("act", exp_cnt[u - 4] if u >= 4 else expa)]
                        smm = S.add("tensor", lambda ee, b=b, sl=sl, u=u,
                                    qoff=qoff: ee.matmul(
                                        out=ps_s[:, 512 * sl:512 * (sl + 1)],
                                        lhsT=kh[:, T * b + 128 * u:T * b + 128 * (u + 1)],
                                        rhs=qh[:, qoff:qoff + 512],
                                        start=True, stop=True),
                                    waits=sw, inc="pe")
                        exp_cnt[u] = S.add("scalar", lambda ee, sl=sl, hl=hl:
                                           ee.activation(
                                               out=pg[:, 512 * sl:512 * (sl + 1)],
                                               in_=ps_s[:, 512 * sl:512 * (sl + 1)],
                                               func=AF.Exp,
                                               bias=bb8[:, hl:hl + 1], scale=SCALE),
                                           waits=[("pe", smm)], inc="act")
                        if u >= 4 * qb:
                            sp_ = u - 4 * qb
                            mw_ = [("act", exp_cnt[u])]
                            if S.cnt.get("_c_mask1", 0) == 0:
                                mw_.append(("dC", S.v("dC")))
                                S.cnt["_c_mask1"] = 1
                            mask_cnt[u] = S.add(
                                "vector", lambda ee, sl=sl, sp_=sp_:
                                ee.tensor_tensor(
                                    out=pg[:, 512 * sl + 128 * sp_:
                                           512 * sl + 128 * (sp_ + 1)],
                                    in0=pg[:, 512 * sl + 128 * sp_:
                                           512 * sl + 128 * (sp_ + 1)],
                                    in1=trimask[:, :], op=ALU.mult),
                                waits=mw_, inc="dve")

                    def emit_AVD(u):
                        sl = u % 4
                        diag = (u >= 4 * qb)
                        avw = [("act", exp_cnt[u])]
                        if diag:
                            avw.append(("dve", mask_cnt[u]))
                        if u == 0:
                            avw.append(("dve", S.cnt.get("_c_yt1", 0)))
                            avw.append(("act", S.cnt.get("_c_lnd", 0)))
                        qlo = 128 * (u - 4 * qb) if diag else 0
                        S.add("tensor", lambda ee, b=b, sl=sl, u=u, qlo=qlo: ee.matmul(
                            out=ps_y[:, qlo:512],
                            lhsT=vh[:, T * b + 128 * u:T * b + 128 * (u + 1)],
                            rhs=pg[:, 512 * sl + qlo:512 * (sl + 1)],
                            start=(u == 0), stop=False),
                            waits=avw)
                        S.add("tensor", lambda ee, sl=sl, qlo=qlo, u=u: ee.matmul(
                            out=ps_d[0:1, qlo:512], lhsT=ones_bf[:, 0:1],
                            rhs=pg[:, 512 * sl + qlo:512 * (sl + 1)],
                            start=(u == 0), stop=False),
                            inc="pe")

                    for u0 in range(min(3, U)):
                        emit_S(u0)
                    for u in range(U):
                        emit_AVD(u)
                        if u + 3 < U:
                            emit_S(u + 3)
                    avd_done = S.v("pe")
                    S.add("scalar", lambda ee: ee.activation(
                        out=lnd[0:1, :], in_=ps_d[0:1, :], func=AF.Ln,
                        bias=0.0), waits=[("pe", avd_done)], inc="act")
                    S.add("scalar", lambda ee: ee.activation(
                        out=sdm[0:1, :], in_=lnd[0:1, :], func=AF.Exp,
                        scale=-1.0), inc="act")
                    S.add("scalar", lambda ee: ee.activation(
                        out=lnd[0:1, :], in_=ps_da[0:1, :], func=AF.Ln,
                        bias=0.0), inc="act")
                    S.add("scalar", lambda ee: ee.activation(
                        out=sdag[0:1, :], in_=lnd[0:1, :], func=AF.Exp,
                        scale=-1.0), inc="act")
                    rec_done = S.add("scalar", lambda ee: ee.activation(
                        out=sdag[0:1, :], in_=sdag[0:1, :], func=AF.Identity,
                        scale=gate_sb[0:1, 0:1]), inc="act")
                    S.cnt["_c_lnd"] = rec_done
                    bw = [("act", rec_done)]
                    S.add("tensor", lambda ee: ee.matmul(
                        out=ps_s[:, 0:512], lhsT=ones_rbf[0:1, :], rhs=sdm[0:1, :],
                        start=True, stop=True), waits=bw, inc="pe")
                    bm2 = S.add("tensor", lambda ee: ee.matmul(
                        out=ps_s[:, 512:1024], lhsT=ones_rbf[0:1, :], rhs=sdag[0:1, :],
                        start=True, stop=True), inc="pe")
                    S.add("vector", lambda ee: ee.tensor_copy(
                        out=sbd[:, :], in_=ps_s[:, 0:512]), waits=[("pe", bm2)],
                        inc="dve")
                    S.cnt["_c_sbd"] = S.add("vector", lambda ee: ee.tensor_copy(
                        out=sbda[:, :], in_=ps_s[:, 512:1024]), inc="dve")
                    S.cnt["_c_yt1"] = S.add("vector", lambda ee: ee.tensor_tensor(
                        out=yt1[:, :], in0=ps_y[:, :], in1=sbd[:, :], op=ALU.mult),
                        inc="dve")
                    S.cnt["_c_yt2"] = S.add("vector", lambda ee: ee.tensor_tensor(
                        out=yt2[:, :], in0=ps_ya[:, :], in1=sbda[:, :], op=ALU.mult),
                        inc="dve")
                    yb = qb % 2
                    dyb = "dy%d" % yb
                    ydw = []
                    prevy = S.cnt.get("_c_ydma_%d_%d" % (hl, qb - 2), 0)
                    if prevy:
                        ydw.append((dyb, prevy))
                    yv = S.add("vector", lambda ee, yb=yb: ee.tensor_tensor(
                        out=ybuf[:, 512 * yb:512 * (yb + 1)], in0=yt1[:, :], in1=yt2[:, :],
                        op=ALU.add), waits=ydw, inc="dve")
                    S.cnt["_c_ydma_%d_%d" % (hl, qb)] = S.add(
                        "gpsimd", lambda ee, hl=hl, qb=qb, yb=yb: ee.dma_start(
                            out=a2a_y_in[512 * (qb // 2) + 128 * hl:
                                         512 * (qb // 2) + 128 * (hl + 1),
                                         512 * (qb % 2):512 * (qb % 2) + 512],
                            in_=ybuf[:, 512 * yb:512 * (yb + 1)]),
                        waits=[("dve", yv)], inc=dyb, n=16)
                S.cnt["_c_hpe_%d" % hl] = S.v("pe")

            def do_a2a(ee):
                with allow_ici_all_to_all():
                    return ee.collective_compute(
                        "AllToAll", ALU.bypass,
                        replica_groups=[list(range(NCORES))],
                        ins=[a2a_y_in.ap().opt()],
                        outs=[a2a_y_out.ap().opt()])
            S.add("gpsimd", do_a2a,
                  waits=[("dy0", S.v("dy0")), ("dy1", S.v("dy1"))], inc="cc")
            emit_block(nc, sems, S, "C")

        with ExitStack() as phd1:
            e1 = phd1.enter_context
            sxn_d = e1(nc.sbuf_tensor("sxn_d", [128, NCC * TOWN], BF16))
            swd2 = e1(nc.sbuf_tensor("swd2", [128, 2 * 2048], I32))
            silo2 = e1(nc.sbuf_tensor("silo2", [128, 2048], I32))
            sihi2 = e1(nc.sbuf_tensor("sihi2", [128, 2048], I32))
            wnib2 = e1(nc.sbuf_tensor("wnib2", [128, 2 * C], BF16))
            fbuf2 = e1(nc.sbuf_tensor("fbuf2", [128, 4 * 512], F32))
            ps_f2 = e1(nc.psum_tensor("ps_f2", [128, 2 * 512], F32))

            S = Sched(cnt)
            xl = S.add("sync", lambda ee: ee.dma_start(
                out=sxn_d.ap().rearrange("p (a t) -> p a t", a=NCC),
                in_=ag_xn_in.ap().rearrange("(a p) t -> p a t", p=128)),
                waits=[("dsp", S.v("dsp"))], inc="dxn", n=16)

            def ffn_chunk_d(S, oc, slot):
                b = slot % 2
                dwb = "dw%d" % b
                w0 = []
                prev_dve = S.cnt.get("_f2_dve_%d" % (slot - 2), 0)
                if prev_dve:
                    w0.append(("dve", prev_dve))
                wld = S.add("gpsimd", lambda ee, oc=oc, b=b: ee.dma_start(
                    out=swd2.ap()[:, 2048 * b:2048 * (b + 1)].rearrange(
                        "p (a o) -> p a o", a=16),
                    in_=wffn_pk.ap()[2048 * oc:2048 * (oc + 1), :].rearrange(
                        "(a p) o -> p a o", p=128)), waits=w0, inc=dwb, n=16)
                lo = S.add("vector", lambda ee, b=b: ee.tensor_scalar(
                    out=silo2[:, :], in0=swd2[:, 2048 * b:2048 * (b + 1)],
                    scalar1=15, scalar2=None, op0=ALU.bitwise_and),
                    waits=[(dwb, wld)], inc="dve")
                hi = S.add("vector", lambda ee, b=b: ee.tensor_scalar(
                    out=sihi2[:, :], in0=swd2[:, 2048 * b:2048 * (b + 1)],
                    scalar1=4, scalar2=15, op0=ALU.logical_shift_right,
                    op1=ALU.bitwise_and), inc="dve")
                pw = [("pe", S.cnt.get("_f2_pe_%d" % (slot - 2), 0))]
                S.add("vector", lambda ee, b=b: ee.tensor_copy(
                    out=wnib2[:, C * b:C * b + 2048], in_=silo2[:, :]), waits=pw,
                    inc="dve")
                ch = S.add("vector", lambda ee, b=b: ee.tensor_copy(
                    out=wnib2[:, C * b + 2048:C * b + C], in_=sihi2[:, :]),
                    inc="dve")
                last_pe = 0
                for tt in range(2):
                    prev_act = S.cnt.get("_f2_actt_%d_%d" % (slot - 1, tt), 0)
                    for cc2 in range(NCC):
                        w = []
                        if cc2 == 0:
                            w = [("dve", ch), ("act", prev_act), ("dxn", xl)]
                        S.add("tensor", lambda ee, b=b, tt=tt, cc2=cc2: ee.matmul(
                            out=ps_f2[:, 512 * tt:512 * (tt + 1)],
                            lhsT=wnib2[:, C * b + 128 * cc2:C * b + 128 * (cc2 + 1)],
                            rhs=sxn_d[:, TOWN * cc2 + 512 * tt:TOWN * cc2 + 512 * (tt + 1)],
                            start=(cc2 == 0), stop=False), waits=w)
                    last_pe = S.add("tensor", lambda ee, oc=oc, tt=tt: ee.matmul(
                        out=ps_f2[:, 512 * tt:512 * (tt + 1)],
                        lhsT=zp_sb[0:1, 128 * oc:128 * (oc + 1)],
                        rhs=neg_xsum[0:1, 512 * tt:512 * (tt + 1)],
                        start=False, stop=True), inc="pe")
                    fr = 2 * b + tt
                    dof = "do%d" % fr
                    av = S.add("scalar", lambda ee, oc=oc, fr=fr, tt=tt: ee.activation(
                        out=fbuf2[:, 512 * fr:512 * (fr + 1)],
                        in_=ps_f2[:, 512 * tt:512 * (tt + 1)], func=AF.Identity,
                        scale=scale_sb[:, oc:oc + 1], bias=bias_sb[:, oc:oc + 1]),
                        waits=[("pe", last_pe),
                               (dof, S.cnt.get("_f2_dma_%d_%d" % (slot - 2, tt), 0))],
                        inc="act")
                    S.cnt["_f2_actt_%d_%d" % (slot, tt)] = av
                    S.cnt["_f2_dma_%d_%d" % (slot, tt)] = S.add(
                        "sync", lambda ee, oc=oc, fr=fr, tt=tt: ee.dma_start(
                            out=ffn_dram[128 * oc:128 * (oc + 1), 512 * tt:512 * (tt + 1)],
                            in_=fbuf2[:, 512 * fr:512 * (fr + 1)]),
                        waits=[("act", av)], inc=dof, n=16)
                S.cnt["_f2_pe_%d" % slot] = last_pe
                S.cnt["_f2_dve_%d" % slot] = hi

            for i_, oc in enumerate(range(16, 32)):
                ffn_chunk_d(S, oc, i_)
            if dbg:
                S.add("gpsimd", lambda ee: ee.dma_start(
                    out=dbg_t["dbg_ffn2"][:, :], in_=ffn_dram[:, :]),
                    waits=[("do%d" % r, S.v("do%d" % r)) for r in range(4)],
                    inc="dD", n=16)
                S.add("gpsimd", lambda ee: ee.dma_start(
                    out=dbg_t["dbg_sxnd"][:, :], in_=sxn_d[:, :]), inc="dD", n=16)
                S.add("gpsimd", None, waits=[("dD", S.v("dD"))])
            emit_block(nc, sems, S, "D1")

        with ExitStack() as phd2:
            e2 = phd2.enter_context
            syt = e2(nc.sbuf_tensor("syt", [128, NCC * TOWN], BF16))
            swp = e2(nc.sbuf_tensor("swp", [128, 2 * C], BF16))
            sfn = e2(nc.sbuf_tensor("sfn", [128, 2 * TOWN], F32))
            sxr = e2(nc.sbuf_tensor("sxr", [128, 2 * TOWN], F32))
            pt1 = e2(nc.sbuf_tensor("pt1", [128, 512], F32))
            obuf = e2(nc.sbuf_tensor("obuf", [128, 2 * 512], F32))
            ps_p = e2(nc.psum_tensor("ps_p", [128, 2 * 512], F32))

            S = Sched(cnt)
            ffn_fin = [("do%d" % r, S.v("do%d" % r)) for r in range(4)]
            yl = S.add("sync", lambda ee: ee.dma_start(
                out=syt.ap().rearrange("p (a t) -> p a t", a=NCC),
                in_=a2a_y_out.ap().rearrange("(a p) t -> p a t", p=128)),
                waits=[("cc", 2)], inc="dyt", n=16)
            for oc in range(32):
                b = oc % 2
                dpb = "dp%d" % b
                dfb = "df%d" % b
                w = []
                if oc >= 2:
                    w.append(("pe", S.cnt["_d2_pe_%d" % (oc - 2)]))
                wl = S.add("gpsimd", lambda ee, oc=oc, b=b: ee.dma_start(
                    out=swp.ap()[:, C * b:C * (b + 1)].rearrange("p (a o) -> p a o",
                                                                 a=NCC),
                    in_=wprojT.ap()[C * oc:C * (oc + 1), :].rearrange(
                        "(a p) o -> p a o", p=128)), waits=w, inc=dpb, n=16)
                w2 = list(ffn_fin) if oc == 0 else []
                if oc >= 2:
                    w2.append(("dve", S.cnt["_d2_dve_%d" % (oc - 2)]))
                fl = S.add("sync", lambda ee, oc=oc, b=b: ee.dma_start(
                    out=sfn[:, TOWN * b:TOWN * (b + 1)],
                    in_=ffn_dram[128 * oc:128 * (oc + 1), :]), waits=w2, inc=dfb, n=16)
                fl = S.add("sync", lambda ee, oc=oc, b=b: ee.dma_start(
                    out=sxr[:, TOWN * b:TOWN * (b + 1)],
                    in_=xT_nat[128 * oc:128 * (oc + 1), :]), inc=dfb, n=16)
                for tt in range(2):
                    pw = [(dpb, wl), ("dyt", yl)]
                    prev_ov = S.cnt.get("_d2_ov_%d_%d" % (oc - 1, tt), 0)
                    if prev_ov:
                        pw.append(("dve", prev_ov))
                    for cc in range(NCC):
                        S.add("tensor", lambda ee, b=b, tt=tt, cc=cc: ee.matmul(
                            out=ps_p[:, 512 * tt:512 * (tt + 1)],
                            lhsT=swp[:, C * b + 128 * cc:C * b + 128 * (cc + 1)],
                            rhs=syt[:, TOWN * cc + 512 * tt:TOWN * cc + 512 * (tt + 1)],
                            start=(cc == 0), stop=(cc == NCC - 1)),
                            waits=pw if cc == 0 else (),
                            inc=("pe" if cc == NCC - 1 else None))
                    mmn = S.v("pe")
                    ew = [("pe", mmn), (dfb, fl)]
                    dzb = "dz%d" % tt
                    prev_od = S.cnt.get("_d2_odma_%d_%d" % (oc - 1, tt), 0)
                    if prev_od:
                        ew.append((dzb, prev_od))
                    S.add("vector", lambda ee, b=b, tt=tt: ee.tensor_tensor(
                        out=pt1[:, :], in0=ps_p[:, 512 * tt:512 * (tt + 1)],
                        in1=sfn[:, TOWN * b + 512 * tt:TOWN * b + 512 * (tt + 1)],
                        op=ALU.add), waits=ew)
                    ov = S.add("vector", lambda ee, b=b, tt=tt: ee.tensor_tensor(
                        out=obuf[:, 512 * tt:512 * (tt + 1)], in0=pt1[:, :],
                        in1=sxr[:, TOWN * b + 512 * tt:TOWN * b + 512 * (tt + 1)],
                        op=ALU.add), inc="dve")
                    S.cnt["_d2_ov_%d_%d" % (oc, tt)] = ov
                    S.cnt["_d2_odma_%d_%d" % (oc, tt)] = S.add(
                        "sync", lambda ee, oc=oc, tt=tt: ee.dma_start(
                            out=outT[128 * oc:128 * (oc + 1), 512 * tt:512 * (tt + 1)],
                            in_=obuf[:, 512 * tt:512 * (tt + 1)]),
                        waits=[("dve", ov)], inc=dzb, n=16)
                S.cnt["_d2_pe_%d" % oc] = S.v("pe")
                S.cnt["_d2_dve_%d" % oc] = S.v("dve")

            if dbg:
                alldone = [("dz0", S.v("dz0")), ("dz1", S.v("dz1"))] + \
                          [("do%d" % r, S.v("do%d" % r)) for r in range(4)] + [
                           ("dsp", S.v("dsp")), ("cc", 2),
                           ("dr0", S.v("dr0")), ("dr1", S.v("dr1")),
                           ("dr2", S.v("dr2")), ("dr3", S.v("dr3")),
                           ("dv0", S.v("dv0")), ("dv1", S.v("dv1"))]
                pairs = [(dbg_t["dbg_xn"], ag_xn_in), (dbg_t["dbg_q"], qT_sp),
                         (dbg_t["dbg_k"], kT_sp), (dbg_t["dbg_v"], v_sp),
                         (dbg_t["dbg_y"], a2a_y_out), (dbg_t["dbg_ffn"], ffn_dram)]
                first = True
                for dst, src in pairs:
                    S.add("gpsimd", lambda ee, dst=dst, src=src: ee.dma_start(
                        out=dst[:, :], in_=src[:, :]),
                        waits=alldone if first else (), inc="dD", n=16)
                    first = False
                S.add("gpsimd", lambda ee: ee.dma_start(
                    out=dbg_t["dbg_akv"][:, :], in_=sakv[:, :]), inc="dD", n=16)
                S.add("gpsimd", lambda ee: ee.dma_start(
                    out=dbg_t["dbg_b"][:, :], in_=bb8[:, :]), inc="dD", n=16)
                S.add("gpsimd", lambda ee: ee.dma_start(
                    out=dbg_t["dbg_nrm"][0:1, :], in_=qknorm[0:1, :]), inc="dD", n=16)
                S.add("gpsimd", lambda ee: ee.dma_start(
                    out=dbg_t["dbg_nrm"][1:2, 0:4], in_=aknorm[0:1, :]), inc="dD", n=16)
                S.add("gpsimd", lambda ee: ee.dma_start(
                    out=dbg_t["dbg_akt"][:, :], in_=sakT[:, :]), inc="dD", n=16)
                S.add("gpsimd", None, waits=[("dD", S.v("dD"))])

            S.add("sync", None, waits=[("dz0", S.v("dz0")), ("dz1", S.v("dz1"))])
            emit_block(nc, sems, S, "D2")

    return nc


def _stage(inputs):
    x = np.asarray(inputs["L0_attn_x"])
    watt = np.asarray(inputs["L0_attn_w_attn"])
    wproj = np.asarray(inputs["L0_attn_w_proj"])
    ad = np.asarray(inputs["L0_attn_adapter_wte"])
    gat = np.asarray(inputs["L0_attn_gating"])
    pk = np.asarray(inputs["L0_ffn_weight_packed"])
    sc = np.asarray(inputs["L0_ffn_scale"])
    zp = np.asarray(inputs["L0_ffn_zero_point"])
    bi = np.asarray(inputs["L0_ffn_bias"])

    permc = np.r_[np.arange(0, C, 2), np.arange(1, C, 2)]
    xT = np.ascontiguousarray(x[0].T.astype(np.float32))
    xTp = np.ascontiguousarray(xT[permc])

    wffn_st = np.ascontiguousarray(
        pk.reshape(C, 2048).T.reshape(2048, 32, 128).transpose(1, 0, 2)
    ).reshape(32 * 2048, 128).astype(np.int32)
    wprojT_st = np.ascontiguousarray(
        wproj.reshape(32, 128, C).transpose(0, 2, 1)).reshape(32 * C, 128).astype(np.float32)
    scale_st = np.ascontiguousarray(sc.reshape(32, 128).T).astype(np.float32)
    bias_st = np.ascontiguousarray(bi.reshape(32, 128).T).astype(np.float32)
    zp_st = np.ascontiguousarray(zp.reshape(1, C)).astype(np.float32)
    adT_st = np.ascontiguousarray(
        ad.T.astype(np.float32)[permc].reshape(32, 128, 10).transpose(1, 0, 2)
    ).reshape(128, 320)
    gat_st = np.ascontiguousarray(gat.reshape(1, 1)).astype(np.float32)

    jj = np.arange(64)
    theta = 10000.0 ** (-jj / 64.0)
    ang = np.outer(theta, np.arange(T).astype(np.float64))
    cosT = np.concatenate([np.cos(ang), np.cos(ang)], 0).astype(NPBF16)
    sinT = np.concatenate([-np.sin(ang), np.sin(ang)], 0).astype(NPBF16)
    trim = np.triu(np.ones((128, 128))).astype(NPBF16)
    iden = np.eye(128).astype(NPBF16)

    in_maps = []
    for i in range(NCORES):
        q_rows = watt[512 * i:512 * (i + 1)]
        k_rows = watt[C + 512 * i:C + 512 * (i + 1)]
        v_rows = watt[2 * C + 512 * i:2 * C + 512 * (i + 1)]
        qkT = np.concatenate([q_rows, k_rows], 0).T[permc]
        watt_qkT = np.ascontiguousarray(
            qkT.reshape(C, 8, 128).transpose(1, 0, 2)).reshape(8 * C, 128).astype(np.float32)
        wvT_st = np.ascontiguousarray(v_rows.T[permc]).astype(np.float32)
        wakT_st = np.ascontiguousarray(k_rows.T[permc]).astype(np.float32)
        in_maps.append({
            "xT_perm": np.ascontiguousarray(xTp[:, TOWN * i:TOWN * (i + 1)]),
            "xT_nat": np.ascontiguousarray(xT[:, TOWN * i:TOWN * (i + 1)]),
            "watt_qkT": watt_qkT,
            "wvT": wvT_st,
            "wakT": wakT_st,
            "wprojT": wprojT_st,
            "wffn_pk": wffn_st,
            "zp_row": zp_st,
            "scale_pc": scale_st,
            "bias_pc": bias_st,
            "adT": adT_st,
            "gating_in": gat_st,
            "cosT_in": cosT,
            "sinT_in": sinT,
            "trimask_in": trim,
            "ident_in": iden,
        })
    return in_maps


def kernel(**inputs):
    in_maps = _stage(inputs)
    SCHED_TRACE.clear()
    nc = build(dbg=DBG)
    check_deadlock()
    res = run_bass_kernel_spmd(nc, in_maps, core_ids=list(range(NCORES)))
    out = np.empty((1, T, C), dtype=np.float32)
    for i in range(NCORES):
        out[0, TOWN * i:TOWN * (i + 1), :] = res.results[i]["outT"].T
    if DBG:
        kernel._dbg = res.results
    return out
